# version 06
# baseline (speedup 1.0000x reference)
import numpy as np
import concourse.bass as bass
import concourse.mybir as mybir
from concourse.bass_utils import run_bass_kernel_spmd

F32 = mybir.dt.float32
BF16 = mybir.dt.bfloat16
AF = mybir.ActivationFunctionType
ALU = mybir.AluOpType
AX = mybir.AxisListType

BC, T, D, H, DK, SEG = 2, 2048, 1024, 16, 64, 256
NSB, KC = T // SEG, D // 128
EPS_ATT, EPS_LN, EPS_FFN = 1e-6, 1e-5, 1e-8
NEG = -30000.0


def out_off(h):
    if h < 7:
        return ("o", 65 * h)
    if h < 14:
        return ("o", 512 + 65 * (h - 7))
    return ("b", 65 * (h - 14))


def build(dbg_specs=(), n_units=16, ph=99):
    nc = bass.Bass("TRN2", target_bir_lowering=False)

    x_d = nc.dram_tensor("x", [BC, T, D], F32, kind="ExternalInput").ap()
    w_d = {nm: nc.dram_tensor(nm, [D, D], F32, kind="ExternalInput").ap()
           for nm in ("wqt", "wkt", "wvt", "wot", "wut", "wft", "wfn")}
    nw_d = nc.dram_tensor("nw", [D], F32, kind="ExternalInput").ap()
    fb_d = nc.dram_tensor("fb", [D], F32, kind="ExternalInput").ap()
    out_d = nc.dram_tensor("out", [BC, T, D], F32, kind="ExternalOutput").ap()
    scr_g = nc.dram_tensor("scr_g", [8, T], F32, kind="Internal").ap()
    scr_ws = nc.dram_tensor("scr_ws", [2, D], F32, kind="Internal").ap()
    dbg_d = {nm: nc.dram_tensor("dbg_" + nm, list(shape), dt, kind="ExternalOutput").ap()
             for nm, shape, dt in dbg_specs}
    dbg_done = []

    S = {}
    def al(name, shape, dtype=F32):
        S[name] = nc.alloc_sbuf_tensor(name, list(shape), dtype)

    for w in ("wq", "wk", "wv", "wu", "wo", "wf"):
        al(w, [128, KC * D], BF16)
    al("stack", [128, 16 * H * 65], BF16)
    for t in ("xnf", "xn2f", "xqf", "pqf", "pkf", "uf", "pktm"):
        al(t, [128, KC * SEG], BF16)
    al("vtm", [128, 2 * H * 65], BF16)
    al("xsb", [128, 2 * D], F32)
    al("xn_t", [128, D], F32)
    al("zone", [128, 6144], F32)
    Z = S["zone"]
    S["ystrip"] = Z[:, 0:2048]
    S["sd"] = Z[:, 2048:4096]
    S["d0"] = Z[:, 4096:5120]
    S["d1"] = Z[:, 5120:6144]
    S["ep_a"] = Z[:, 0:1024]
    S["ep_e"] = Z[:, 1024:2048]
    S["ep_f"] = Z[:, 2048:3072]
    S["ep_g"] = Z[:, 3072:4096]
    S["hb"] = Z[:, 4096:5120]
    S["elu1"] = Z[:, 5120:5632]
    S["elu2"] = Z[:, 5632:6144]
    S["rwsb"] = Z[:, 5120:6144]
    S["arf"] = S["xn2f"]
    S["ar"] = S["uf"][:]
    S["atb"] = Z[:, 4096:4352].bitcast(BF16)
    S["atm"] = Z[:, 4352:4608].bitcast(BF16)
    S["s2m"] = Z[:, 4608:4800].bitcast(BF16)
    S["gcm"] = bass.AP(Z[:, 4800:5056].tensor, Z[:, 4800:5056].offset,
                       [[1, 16], [1, 256]]) if False else Z[0:16, 4800:5056]
    S["gstr"] = Z[:, 5888:6144].bitcast(BF16)
    S["iotc"] = Z[:, 1024:1152]
    al("iotp", [128, 1], F32)
    al("ident", [128, 128], F32)
    al("identb", [128, 128], BF16)
    al("diag", [128, 128], F32)
    al("maskA", [128, 128], BF16)
    al("stats", [128, 16], F32)
    al("mst", [128, KC * 8], BF16)
    al("aw", [128, 32], F32)
    al("awp", [128, 16], F32)
    al("nwv", [128, KC], F32)
    al("bbc", [128, D], F32)
    al("wsb", [128, D], F32)
    al("dvec", [128, 16], F32)
    al("rvec", [128, 16], F32)
    al("vsc", [128, 32], F32)
    al("sc", [128, 32], F32)
    al("wsv", [128, KC], F32)
    al("cb", [128, 4], F32)

    def pgt_buf(nb):
        return S["uf"] if nb == 0 else S["xnf"]

    ps_scum = nc.alloc_psum_tensor("ps_scum", [64, 2048], F32)
    ps_a = nc.alloc_psum_tensor("ps_a", [128, 512], F32)
    ps_b = nc.alloc_psum_tensor("ps_b", [128, 512], F32)
    ps_o = nc.alloc_psum_tensor("ps_o", [128, 1024], F32)
    PS = {"a": ps_a, "b": ps_b, "o": ps_o}

    class Ctr:
        def __init__(self, name):
            self.h = nc.semaphore(name).__enter__()
            self.v = 0
    (ld, gd, pe, dv, ac, gp, so, xl, wa, wb, bc, wt, gw, da, db, ga, gb) = (
        Ctr(n) for n in ("ld", "gd", "pe", "dv", "ac", "gp", "so", "xl", "wa",
                         "wb", "bc", "wt", "gw", "da", "db", "ga", "gb"))

    blk = nc.Block().__enter__()
    EV, EA, EP, EG, ES = blk.vector, blk.scalar, blk.tensor, blk.gpsimd, blk.sync

    def W(e, *pairs):
        for ctr, val in pairs:
            if val > 0:
                e.wait_ge(ctr.h, val)

    psfree = {"a": None, "b": None, "o": None, "scum": None}
    def bank_wait(e, *banks):
        for bk in banks:
            if psfree[bk] is not None:
                W(e, psfree[bk])

    def _s0(sync):
        sync.dma_start(out=S["nwv"][:], in_=nw_d.rearrange("(c p) -> p c", p=128),
                       allow_slow_non_contiguous=True).then_inc(bc.h, 16)
        sync.dma_start(out=S["bbc"][:], in_=fb_d.unsqueeze(0).broadcast_to([128, D])).then_inc(bc.h, 16)
    ES(_s0); bc.v += 32

    def _g0(gpsimd):
        for w, dn in (("wq", "wqt"), ("wk", "wkt"), ("wv", "wvt"), ("wu", "wut"), ("wo", "wot")):
            gpsimd.dma_start(out=S[w][:].rearrange("p (c d) -> p c d", d=D),
                             in_=w_d[dn].rearrange("(c p) d -> p c d", p=128)).then_inc(wt.h, 16)
        gpsimd.iota(S["iotc"][:], pattern=[[1, 128]], base=0, channel_multiplier=0,
                    allow_small_or_imprecise_dtypes=True)
        gpsimd.iota(S["iotp"][:], pattern=[[0, 1]], base=0, channel_multiplier=1,
                    allow_small_or_imprecise_dtypes=True)
        gpsimd.drain().then_inc(gp.h, 1)
    EG(_g0); wt.v += 80; gp.v += 1
    t_wload, t_iota = wt.v, gp.v

    def _v0(vector):
        vector.wait_ge(gp.h, t_iota)
        vector.tensor_scalar(out=S["ident"][:], in0=S["iotc"][:], scalar1=S["iotp"][:],
                             scalar2=None, op0=ALU.subtract)
        vector.drain()
        vector.tensor_scalar(out=S["maskA"][:], in0=S["ident"][:], scalar1=0.0,
                             scalar2=None, op0=ALU.is_ge)
        vector.tensor_scalar(out=S["ident"][:], in0=S["ident"][:], scalar1=0.0,
                             scalar2=None, op0=ALU.is_equal)
        vector.drain()
        vector.tensor_copy(out=S["identb"][:], in_=S["ident"][:])
        vector.tensor_scalar(out=S["vsc"][:, 2:3], in0=S["iotp"][:], scalar1=1.0,
                             scalar2=32.0, op0=ALU.add, op1=ALU.mult)
        vector.tensor_scalar(out=S["vsc"][:, 3:4], in0=S["iotp"][:], scalar1=129.0,
                             scalar2=32.0, op0=ALU.add, op1=ALU.mult)
        vector.drain()
        vector.reciprocal(out=S["vsc"][:, 0:1], in_=S["vsc"][:, 2:3])
        vector.reciprocal(out=S["vsc"][:, 1:2], in_=S["vsc"][:, 3:4])
        for blki in range(2):
            for h in range(H):
                vector.memset(S["vtm"][:, blki * 1040 + 65 * h + 64: blki * 1040 + 65 * h + 65], 1.0)
        vector.memset(S["awp"][:], 0.0)
        vector.memset(S["cb"][:, 0:1], EPS_LN)
        vector.memset(S["cb"][:, 1:2], EPS_ATT)
        vector.memset(S["cb"][:, 2:3], 128.5)
        vector.drain().then_inc(dv.h, 1)
    EV(_v0); dv.v += 1
    t_consts = dv.v

    t_rd = {}
    for cc in range(KC):
        half = cc % 2
        wch = wa if half == 0 else wb
        def _wl(sync, cc=cc, half=half, wch=wch):
            if cc >= 2:
                sync.wait_ge(dv.h, t_rd[cc - 2])
            sync.dma_start(out=S["xsb"][:, half * D: (half + 1) * D],
                           in_=w_d["wfn"][cc * 128: (cc + 1) * 128, :]).then_inc(wch.h, 16)
        ES(_wl); wch.v += 16
        t_wl = wch.v

        def _wr(vector, cc=cc, half=half, t_wl=t_wl, wch=wch):
            vector.wait_ge(wch.h, t_wl)
            vector.tensor_reduce(out=S["wsv"][:, cc: cc + 1],
                                 in_=S["xsb"][:, half * D: (half + 1) * D],
                                 axis=AX.X, op=ALU.max, apply_absolute_value=True)
            vector.drain().then_inc(dv.h, 1)
        EV(_wr); dv.v += 1
        t_rd[cc] = dv.v

    def _wq1(vector):
        vector.drain()
        vector.tensor_scalar(out=S["wsv"][:], in0=S["wsv"][:], scalar1=EPS_FFN,
                             scalar2=1.0 / 127.0, op0=ALU.max, op1=ALU.mult)
        vector.drain()
        vector.reciprocal(out=S["vsc"][:, 16:24], in_=S["wsv"][:])
        vector.drain().then_inc(dv.h, 1)
    EV(_wq1); dv.v += 1
    t_wq1 = dv.v

    def _wq2(sync):
        sync.wait_ge(dv.h, t_wq1)
        sync.dma_start(out=scr_ws[1:2, :].rearrange("o (c p) -> p (o c)", p=128),
                       in_=S["wsv"][:], allow_slow_non_contiguous=True).then_inc(bc.h, 16)
        sync.dma_start(out=scr_ws[0:1, :].rearrange("o (c p) -> p (o c)", p=128),
                       in_=S["vsc"][:, 16:24], allow_slow_non_contiguous=True).then_inc(bc.h, 16)
        sync.wait_ge(bc.h, bc.v + 32)
        sync.dma_start(out=S["wsb"][:], in_=scr_ws[1:2, :].broadcast_to([128, D])).then_inc(bc.h, 16)
        sync.dma_start(out=S["rwsb"][:], in_=scr_ws[0:1, :].broadcast_to([128, D])).then_inc(bc.h, 16)
    ES(_wq2); bc.v += 64
    t_wq2 = bc.v

    t_wq3c = {}
    for cc in range(KC):
        half = cc % 2
        wch = wa if half == 0 else wb
        def _wl2(sync, cc=cc, half=half, wch=wch):
            if cc >= 2:
                sync.wait_ge(dv.h, t_wq3c[cc - 2])
            sync.dma_start(out=S["xsb"][:, half * D: (half + 1) * D],
                           in_=w_d["wft"][cc * 128: (cc + 1) * 128, :]).then_inc(wch.h, 16)
        ES(_wl2); wch.v += 16
        t_wl2 = wch.v

        def _wq3(vector, cc=cc, half=half, t_wl2=t_wl2, t_wq2=t_wq2, wch=wch):
            vector.wait_ge(wch.h, t_wl2)
            vector.wait_ge(bc.h, t_wq2)
            src = S["xsb"][:, half * D: (half + 1) * D]
            y, r, d_ = S["ystrip"][:, 0:D], S["sd"][:, 0:D], S["d0"][:]
            vector.tensor_tensor(out=y, in0=src, in1=S["rwsb"][:], op=ALU.mult)
            vector.drain()
            vector.tensor_scalar(out=r, in0=y, scalar1=128.5, scalar2=None, op0=ALU.add)
            vector.drain()
            for k in range(7, -1, -1):
                sv = float(1 << k)
                vector.tensor_scalar(out=d_, in0=r, scalar1=sv, scalar2=-sv,
                                     op0=ALU.is_ge, op1=ALU.mult)
                vector.drain()
                vector.tensor_tensor(out=r, in0=r, in1=d_, op=ALU.add)
                vector.drain()
            vector.scalar_tensor_tensor(out=S["wf"][:, cc * D: (cc + 1) * D],
                                        in0=y, scalar=0.5, in1=r, op0=ALU.add, op1=ALU.subtract)
            vector.drain().then_inc(dv.h, 1)
        EV(_wq3); dv.v += 1
        t_wq3c[cc] = dv.v
    tk0_wq3 = dv.v

    tk = {"xsb_free_ac": 0, "xsb_free_dv": 0, "xn_t_free_pe": 0, "xn_t_free_dv": 0,
          "xrl_free": 0, "epf_free_ac": 0, "strip_free": 0,
          "dup": {}, "cp": {}}

    def st(tb, j):
        return S["stats"][:, tb * 8 + j: tb * 8 + j + 1]

    units = [(b, si) for b in range(BC) for si in range(NSB)][:n_units]
    for b, si in units:
        if True:
            c = si
            tok0 = si * SEG
            pe_prev = pe.v

            def _ph0(sync, b=b, tok0=tok0, w1=tk["xsb_free_ac"],
                     w2=max(tk["xsb_free_dv"], tk0_wq3 if (b == 0 and si == 0) else 0)):
                if w1:
                    sync.wait_ge(ac.h, w1)
                if w2:
                    sync.wait_ge(dv.h, w2)
                sync.dma_start(
                    out=S["xsb"][:].rearrange("p (tb d) -> p tb d", d=D),
                    in_=x_d[b, tok0: tok0 + SEG, :].rearrange("(tb p) d -> p tb d", p=128),
                ).then_inc(xl.h, 16)
            ES(_ph0); xl.v += 16
            t_x = xl.v

            def _p1v(vector, t_x=t_x):
                W(vector, (xl, t_x), (dv, t_consts), (bc, t_wq2))
                for tb in range(2):
                    vector.tensor_reduce(out=S["vsc"][:, 4 + tb: 5 + tb],
                                         in_=S["xsb"][:, tb * D: (tb + 1) * D], axis=AX.X, op=ALU.add)
                vector.drain()
                for tb in range(2):
                    vector.tensor_scalar(out=st(tb, 0), in0=S["vsc"][:, 4 + tb: 5 + tb],
                                         scalar1=-1.0 / D, scalar2=None, op0=ALU.mult)
                vector.drain().then_inc(dv.h, 1)
            EV(_p1v); dv.v += 1
            t_mean = dv.v

            def _p1a(scalar, t_x=t_x, t_mean=t_mean, wdv=tk["strip_free"]):
                W(scalar, (xl, t_x), (dv, t_mean))
                if wdv:
                    scalar.wait_ge(dv.h, wdv)
                for tb in range(2):
                    scalar.activation(out=S["ep_f" if tb == 0 else "ep_g"][:],
                                      in_=S["xsb"][:, tb * D: (tb + 1) * D],
                                      func=AF.Square, accum_out=S["vsc"][:, 6 + tb: 7 + tb])
                scalar.drain().then_inc(ac.h, 1)
            EA(_p1a); ac.v += 1
            t_sq = ac.v

            def _p1v2(vector, t_sq=t_sq):
                W(vector, (ac, t_sq))
                for tb in range(2):
                    vector.tensor_scalar(out=S["vsc"][:, 8 + tb: 9 + tb], in0=st(tb, 0),
                                         scalar1=st(tb, 0), scalar2=None, op0=ALU.mult)
                    vector.tensor_scalar(out=S["vsc"][:, 10 + tb: 11 + tb],
                                         in0=S["vsc"][:, 6 + tb: 7 + tb],
                                         scalar1=1.0 / D, scalar2=None, op0=ALU.mult)
                vector.drain()
                for tb in range(2):
                    vector.tensor_tensor(out=S["vsc"][:, 12 + tb: 13 + tb],
                                         in0=S["vsc"][:, 10 + tb: 11 + tb],
                                         in1=S["vsc"][:, 8 + tb: 9 + tb], op=ALU.subtract)
                vector.drain().then_inc(dv.h, 1)
            EV(_p1v2); dv.v += 1
            t_var = dv.v

            def _p1a2(scalar, t_var=t_var):
                W(scalar, (dv, t_var))
                for tb in range(2):
                    scalar.activation(out=S["vsc"][:, 14 + tb: 15 + tb],
                                      in_=S["vsc"][:, 12 + tb: 13 + tb], func=AF.Ln, bias=S["cb"][:, 0:1])
                scalar.drain()
                for tb in range(2):
                    scalar.activation(out=st(tb, 1), in_=S["vsc"][:, 14 + tb: 15 + tb],
                                      func=AF.Exp, scale=-0.5)
                scalar.drain().then_inc(ac.h, 1)
            EA(_p1a2); ac.v += 1
            t_sd_ = ac.v

            def _p1v3(vector, t_sd_=t_sd_):
                W(vector, (ac, t_sd_))
                for tb in range(2):
                    vector.tensor_scalar(out=st(tb, 5), in0=st(tb, 0), scalar1=st(tb, 1),
                                         scalar2=None, op0=ALU.mult)
                vector.drain().then_inc(dv.h, 1)
            EV(_p1v3); dv.v += 1
            t_istd = dv.v

            def _p1v3b(vector, t_istd=t_istd):
                for tb in range(2):
                    vector.tensor_scalar(out=S["vsc"][:, 18 + tb: 19 + tb],
                                         in0=S["vsc"][:, 12 + tb: 13 + tb],
                                         scalar1=st(tb, 1), scalar2=st(tb, 1),
                                         op0=ALU.mult, op1=ALU.mult)
                vector.drain().then_inc(dv.h, 1)
            EV(_p1v3b); dv.v += 1
            t_rin = dv.v

            def _p1a4(scalar, t_rin=t_rin):
                W(scalar, (dv, t_rin))
                for tb in range(2):
                    scalar.activation(out=S["vsc"][:, 20 + tb: 21 + tb],
                                      in_=S["vsc"][:, 18 + tb: 19 + tb], func=AF.Ln, bias=S["cb"][:, 1:2])
                scalar.drain()
                for tb in range(2):
                    scalar.activation(out=st(tb, 2), in_=S["vsc"][:, 20 + tb: 21 + tb],
                                      func=AF.Exp, scale=-0.5)
                scalar.drain().then_inc(ac.h, 1)
            EA(_p1a4); ac.v += 1
            t_rmsq = ac.v

            t_xqf_prev = tk["strip_free"]
            t_xn = {}
            for tb in range(2):
                def _xn(scalar, tb=tb, t_istd=t_istd,
                        wpe=tk["xn_t_free_pe"], wdv=tk["xn_t_free_dv"]):
                    W(scalar, (dv, t_istd))
                    if wpe:
                        scalar.wait_ge(pe.h, wpe)
                    if wdv:
                        scalar.wait_ge(dv.h, wdv)
                    scalar.activation(out=S["xn_t"][:], in_=S["xsb"][:, tb * D: (tb + 1) * D],
                                      func=AF.Identity, bias=st(tb, 5), scale=st(tb, 1))
                    scalar.drain().then_inc(ac.h, 1)
                EA(_xn); ac.v += 1
                t_xn[tb] = ac.v

                def _am(vector, tb=tb, t_x2=t_xn[tb]):
                    W(vector, (ac, t_x2))
                    vector.tensor_reduce(out=S["vsc"][:, 16 + tb: 17 + tb], in_=S["xn_t"][:],
                                         axis=AX.X, op=ALU.max, apply_absolute_value=True)
                    vector.drain()
                    vector.tensor_scalar(out=st(tb, 4), in0=S["vsc"][:, 16 + tb: 17 + tb],
                                         scalar1=EPS_FFN, scalar2=1.0 / 127.0,
                                         op0=ALU.max, op1=ALU.mult)
                    vector.drain()
                    vector.reciprocal(out=st(tb, 3), in_=st(tb, 4))
                    vector.drain()
                    vector.tensor_scalar(out=S["vsc"][:, 8:9], in0=st(tb, 1),
                                         scalar1=st(tb, 2), scalar2=None, op0=ALU.mult)
                    vector.tensor_scalar(out=S["vsc"][:, 9:10], in0=st(tb, 5),
                                         scalar1=st(tb, 2), scalar2=None, op0=ALU.mult)
                    vector.tensor_scalar(out=S["vsc"][:, 10:11], in0=st(tb, 1),
                                         scalar1=st(tb, 3), scalar2=None, op0=ALU.mult)
                    vector.tensor_scalar(out=S["vsc"][:, 11:12], in0=st(tb, 5),
                                         scalar1=st(tb, 3), scalar2=128.5,
                                         op0=ALU.mult, op1=ALU.add)
                    vector.drain().then_inc(dv.h, 1)
                EV(_am); dv.v += 1
                t_am = dv.v

                sbuf_sc = S["sd"][:, 0:1024]
                for var in range(3):
                    if var > 0:
                        pe_now = pe.v
                        def _sc(scalar, tb=tb, var=var, t_am=t_am, pe_now=pe_now):
                            W(scalar, (dv, t_am), (pe, pe_now))
                            if var == 1:
                                scalar.activation(out=S["sd"][:, 0:1024],
                                                  in_=S["xsb"][:, tb * D: (tb + 1) * D],
                                                  func=AF.Identity, scale=S["vsc"][:, 8:9],
                                                  bias=S["vsc"][:, 9:10])
                            else:
                                scalar.activation(out=S["sd"][:, 0:1024],
                                                  in_=S["xsb"][:, tb * D: (tb + 1) * D],
                                                  func=AF.Identity, scale=S["vsc"][:, 10:11],
                                                  bias=S["vsc"][:, 11:12])
                            scalar.drain().then_inc(ac.h, 1)
                        EA(_sc); ac.v += 1
                        t_src = ac.v
                        src = sbuf_sc
                    else:
                        t_src = t_xn[tb]
                        src = S["xn_t"][:]
                    for grp in range(2):
                        bank = "a" if ((var * 2 + grp) % 2 == 0) else "b"
                        psu = PS[bank]
                        def _tp(tensor, var=var, tb=tb, grp=grp, psu=psu, src=src,
                                t_src=t_src, bw=psfree[bank]):
                            W(tensor, (ac, t_src))
                            if bw is not None:
                                W(tensor, bw)
                            for j in range(4):
                                kc = grp * 4 + j
                                tensor.transpose(out=psu[:, j * 128: (j + 1) * 128],
                                                 in_=src[:, kc * 128: (kc + 1) * 128],
                                                 identity=S["ident"][:])
                            tensor.drain().then_inc(pe.h, 1)
                        EP(_tp); pe.v += 1
                        t_tp = pe.v

                        if var == 0:
                            def _ev(scalar, tb=tb, grp=grp, psu=psu, t_tp=t_tp):
                                W(scalar, (pe, t_tp))
                                for j in range(4):
                                    kc = grp * 4 + j
                                    scalar.copy(
                                        out=S["xnf"][:, kc * SEG + tb * 128: kc * SEG + tb * 128 + 128],
                                        in_=psu[:, j * 128: (j + 1) * 128])
                                scalar.drain().then_inc(ac.h, 1)
                            EA(_ev); ac.v += 1
                            psfree[bank] = (ac, ac.v)
                        elif var == 1:
                            def _ev(vector, tb=tb, grp=grp, psu=psu, t_tp=t_tp):
                                W(vector, (pe, t_tp))
                                for j in range(4):
                                    kc = grp * 4 + j
                                    vector.tensor_scalar(
                                        out=S["xn2f"][:, kc * SEG + tb * 128: kc * SEG + tb * 128 + 128],
                                        in0=psu[:, j * 128: (j + 1) * 128],
                                        scalar1=S["nwv"][:, kc: kc + 1], scalar2=None, op0=ALU.mult)
                                vector.drain().then_inc(dv.h, 1)
                            EV(_ev); dv.v += 1
                            psfree[bank] = (dv, dv.v)
                        else:
                            def _ev(scalar, tb=tb, grp=grp, psu=psu, t_tp=t_tp, ws=t_xqf_prev,
                                    wso=tk.get("ep_a_free", 0)):
                                W(scalar, (pe, t_tp))
                                if ws:
                                    scalar.wait_ge(dv.h, ws)
                                if wso:
                                    scalar.wait_ge(so.h, wso)
                                scalar.copy(
                                    out=S["ystrip"][:, tb * 1024 + grp * 512: tb * 1024 + grp * 512 + 512],
                                    in_=psu[:])
                                scalar.drain().then_inc(ac.h, 1)
                            EA(_ev); ac.v += 1
                            psfree[bank] = (ac, ac.v)
                tk["xn_t_free_pe"] = pe.v
                tk["xn_t_free_dv"] = t_am
            tk["xsb_free_ac"] = 0
            t_y = ac.v
            def _strip(vector, t_y=t_y, we=tk["epf_free_ac"], wg=tk.get("gwr", 0)):
                W(vector, (ac, t_y))
                if we:
                    vector.wait_ge(ac.h, we)
                if wg:
                    vector.wait_ge(gw.h, wg)
                tt = S["ystrip"][:, 0:2048]
                rr = S["sd"][:, 0:2048]
                dd = S["zone"][:, 4096:6144]
                vector.tensor_copy(out=rr, in_=tt)
                for k in range(7, -1, -1):
                    sv = float(1 << k)
                    vector.drain()
                    vector.tensor_scalar(out=dd, in0=rr, scalar1=sv, scalar2=-sv,
                                         op0=ALU.is_ge, op1=ALU.mult)
                    vector.drain()
                    vector.tensor_tensor(out=rr, in0=rr, in1=dd, op=ALU.add)
                vector.drain()
                vector.tensor_scalar(out=tt, in0=tt, scalar1=-128.0, scalar2=None, op0=ALU.add)
                vector.drain()
                for tb in range(2):
                    tt_src = S["ystrip"][:, tb * 1024: (tb + 1) * 1024].rearrange("p (kc t) -> p kc t", kc=8)
                    rr_src = S["sd"][:, tb * 1024: (tb + 1) * 1024].rearrange("p (kc t) -> p kc t", kc=8)
                    o_ap = S["xqf"][:].rearrange("p (kc t) -> p kc t", kc=8)[:, :, tb * 128: tb * 128 + 128]
                    vector.tensor_tensor(out=o_ap, in0=tt_src, in1=rr_src, op=ALU.subtract)
                vector.drain().then_inc(dv.h, 1)
            EV(_strip); dv.v += 1
            tk["strip_free"] = dv.v
            t_xqf = dv.v

            if ph < 2:
                continue
            t_wq_ready = t_wload
            elu_fr = {"1": 0, "2": 0}
            t_xn2 = dv.v
            t_xnf = ac.v

            for proj, wname, dst in (("q", "wq", "pqf"), ("k", "wk", "pkf")):
                for m in range(KC):
                    bank = "a" if (m % 2 == 0) else "b"
                    psu = PS[bank]
                    def _mm(tensor, wname=wname, m=m, psu=psu, bw=psfree[bank],
                            t_xn2=t_xn2, t_wq_ready=t_wq_ready):
                        W(tensor, (dv, t_xn2), (wt, t_wq_ready))
                        if bw is not None:
                            W(tensor, bw)
                        for k in range(KC):
                            tensor.matmul(out=psu[:, 0:SEG],
                                          lhsT=S[wname][:, k * D + m * 128: k * D + (m + 1) * 128],
                                          rhs=S["xn2f"][:, k * SEG: (k + 1) * SEG],
                                          start=(k == 0), stop=(k == KC - 1))
                        tensor.drain().then_inc(pe.h, 1)
                    EP(_mm); pe.v += 1
                    t_mm = pe.v

                    def _e1(vector, psu=psu, t_mm=t_mm, w1=elu_fr["1"]):
                        W(vector, (pe, t_mm))
                        if w1:
                            vector.wait_ge(ac.h, w1)
                        vector.tensor_scalar(out=S["elu1"][:, 0:SEG], in0=psu[:, 0:SEG],
                                             scalar1=0.0, scalar2=None, op0=ALU.min)
                        vector.drain().then_inc(dv.h, 1)
                    EV(_e1); dv.v += 1
                    t_e1 = dv.v

                    def _e2(scalar, t_e1=t_e1, w2=elu_fr["2"]):
                        W(scalar, (dv, t_e1))
                        if w2:
                            scalar.wait_ge(dv.h, w2)
                        scalar.activation(out=S["elu2"][:, 0:SEG], in_=S["elu1"][:, 0:SEG], func=AF.Exp)
                        scalar.drain().then_inc(ac.h, 1)
                    EA(_e2); ac.v += 1
                    t_e2 = ac.v
                    elu_fr["1"] = t_e2

                    def _e3(vector, psu=psu, m=m, dst=dst, t_e2=t_e2):
                        W(vector, (ac, t_e2))
                        vector.scalar_tensor_tensor(out=S[dst][:, m * SEG: (m + 1) * SEG],
                                                    in0=psu[:, 0:SEG], scalar=0.0,
                                                    in1=S["elu2"][:, 0:SEG],
                                                    op0=ALU.max, op1=ALU.add)
                        vector.drain().then_inc(dv.h, 1)
                    EV(_e3); dv.v += 1
                    elu_fr["2"] = dv.v
                    psfree[bank] = (dv, dv.v)
            t_pq = dv.v
            t_pk = dv.v

            for m in range(KC):
                bank = "a" if (m % 2 == 0) else "b"
                psu = PS[bank]
                def _mmu(tensor, m=m, psu=psu, bw=psfree[bank], t_xnf=t_xnf, t_wq_ready=t_wq_ready):
                    W(tensor, (ac, t_xnf), (wt, t_wq_ready))
                    if bw is not None:
                        W(tensor, bw)
                    for k in range(KC):
                        tensor.matmul(out=psu[:, 0:SEG],
                                      lhsT=S["wu"][:, k * D + m * 128: k * D + (m + 1) * 128],
                                      rhs=S["xnf"][:, k * SEG: (k + 1) * SEG],
                                      start=(k == 0), stop=(k == KC - 1))
                    tensor.drain().then_inc(pe.h, 1)
                EP(_mmu); pe.v += 1
                t_mm = pe.v

                def _evu(scalar, m=m, psu=psu, t_mm=t_mm):
                    W(scalar, (pe, t_mm))
                    scalar.copy(out=S["uf"][:, m * SEG: (m + 1) * SEG], in_=psu[:, 0:SEG])
                    scalar.drain().then_inc(ac.h, 1)
                EA(_evu); ac.v += 1
                psfree[bank] = (ac, ac.v)
            t_uf = ac.v

            for tb in range(2):
                for half in range(2):
                    bank = "a" if ((tb * 2 + half) % 2 == 0) else "b"
                    psu = PS[bank]
                    def _mmv(tensor, tb=tb, half=half, psu=psu, bw=psfree[bank],
                             t_xn2=t_xn2, t_wq_ready=t_wq_ready):
                        W(tensor, (dv, t_xn2), (wt, t_wq_ready))
                        if bw is not None:
                            W(tensor, bw)
                        for k in range(KC):
                            tensor.matmul(out=psu[:],
                                          lhsT=S["xn2f"][:, k * SEG + tb * 128: k * SEG + (tb + 1) * 128],
                                          rhs=S["wv"][:, k * D + half * 512: k * D + half * 512 + 512],
                                          start=(k == 0), stop=(k == KC - 1))
                        tensor.drain().then_inc(pe.h, 1)
                    EP(_mmv); pe.v += 1
                    t_mm = pe.v

                    def _evv(vector, tb=tb, half=half, psu=psu, t_mm=t_mm):
                        W(vector, (pe, t_mm))
                        base = tb * 1040 + half * 520
                        vector.tensor_copy(
                            out=S["vtm"][:, base: base + 520].rearrange(
                                "p (h f) -> p h f", f=65)[:, :, 0:64],
                            in_=psu[:].rearrange("p (h f) -> p h f", f=64))
                        vector.drain().then_inc(dv.h, 1)
                    EV(_evv); dv.v += 1
                    psfree[bank] = (dv, dv.v)
            t_vtm = dv.v

            for step in range(4):
                tb, grp = divmod(step, 2)
                bank = "a" if (step % 2 == 0) else "b"
                psu = PS[bank]
                def _tpk(tensor, tb=tb, grp=grp, psu=psu, bw=psfree[bank], t_pk=t_pk):
                    W(tensor, (dv, t_pk))
                    if bw is not None:
                        W(tensor, bw)
                    pb = psu[:].bitcast(BF16)
                    for j in range(4):
                        rho = grp * 4 + j
                        tensor.transpose(out=pb[:, j * 128: (j + 1) * 128],
                                         in_=S["pkf"][:, rho * SEG + tb * 128: rho * SEG + (tb + 1) * 128],
                                         identity=S["identb"][:])
                    tensor.drain().then_inc(pe.h, 1)
                EP(_tpk); pe.v += 1
                t_mm = pe.v

                def _etk(scalar, tb=tb, grp=grp, psu=psu, t_mm=t_mm):
                    W(scalar, (pe, t_mm))
                    pb = psu[:].bitcast(BF16)
                    for j in range(4):
                        rho = grp * 4 + j
                        scalar.copy(out=S["pktm"][:, tb * 1024 + rho * 128: tb * 1024 + (rho + 1) * 128],
                                    in_=pb[:, j * 128: (j + 1) * 128])
                    scalar.drain().then_inc(ac.h, 1)
                EA(_etk); ac.v += 1
                psfree[bank] = (ac, ac.v)
            t_pktm = ac.v

            def _mst(vector, c=c, t_xnf=t_xnf, wpe=pe_prev):
                W(vector, (ac, t_xnf))
                if wpe:
                    vector.wait_ge(pe.h, wpe)
                with nc.allow_low_precision(reason="m_stack sums stored bf16 as matmul operand"):
                    for kc in range(KC):
                        vector.tensor_reduce(out=S["mst"][:, kc * 8 + c: kc * 8 + c + 1],
                                             in_=S["xnf"][:, kc * SEG: (kc + 1) * SEG],
                                             axis=AX.X, op=ALU.add)
                vector.drain().then_inc(dv.h, 1)
            EV(_mst); dv.v += 1
            t_mst = dv.v

            if ph < 3:
                continue
            def _s2t(tensor, t_xnf=t_xnf, t_uf=t_uf, bw=psfree["a"]):
                W(tensor, (ac, max(t_xnf, t_uf)))
                if bw is not None:
                    W(tensor, bw)
                for k in range(KC):
                    tensor.matmul(out=ps_a[:, 0:256],
                                  lhsT=S["xnf"][:, k * SEG: k * SEG + 128],
                                  rhs=S["uf"][:, k * SEG: (k + 1) * SEG],
                                  start=(k == 0), stop=(k == KC - 1))
                for k in range(KC):
                    tensor.matmul(out=ps_a[:, 256:384],
                                  lhsT=S["xnf"][:, k * SEG + 128: (k + 1) * SEG],
                                  rhs=S["uf"][:, k * SEG + 128: (k + 1) * SEG],
                                  start=(k == 0), stop=(k == KC - 1))
                tensor.drain().then_inc(pe.h, 1)
            EP(_s2t); pe.v += 1
            t_s2t = pe.v

            def _s2e(scalar, t_s2t=t_s2t):
                W(scalar, (pe, t_s2t))
                scalar.copy(out=S["s2m"][:, 0:384], in_=ps_a[:, 0:384])
                scalar.drain().then_inc(ac.h, 1)
            EA(_s2e); ac.v += 1
            t_s2e = ac.v

            def _s2mk(vector, t_s2e=t_s2e):
                W(vector, (ac, t_s2e))
                vector.tensor_tensor(out=S["s2m"][:, 0:128], in0=S["s2m"][:, 0:128],
                                     in1=S["maskA"][:], op=ALU.mult)
                vector.tensor_tensor(out=S["s2m"][:, 256:384], in0=S["s2m"][:, 256:384],
                                     in1=S["maskA"][:], op=ALU.mult)
                vector.drain().then_inc(dv.h, 1)
            EV(_s2mk); dv.v += 1
            t_s2m = dv.v
            psfree["a"] = (ac, t_s2e)

            def _scmm(tensor, c=c, t_s2m=t_s2m, t_mst=t_mst, t_uf=t_uf, bw=psfree["b"]):
                W(tensor, (dv, max(t_s2m, t_mst)), (ac, t_uf))
                if bw is not None:
                    W(tensor, bw)
                ones = S["vtm"][:, 64:65]
                tensor.matmul(out=ps_b[:, 8:9], lhsT=S["s2m"][:, 0:128], rhs=ones,
                              start=True, stop=True)
                tensor.matmul(out=ps_b[:, 24:25], lhsT=S["s2m"][:, 128:256], rhs=ones,
                              start=True, stop=False, skip_group_check=True)
                tensor.matmul(out=ps_b[:, 24:25], lhsT=S["s2m"][:, 256:384], rhs=ones,
                              start=False, stop=True, skip_group_check=True)
                if c > 0:
                    for tb in range(2):
                        for k in range(KC):
                            tensor.matmul(out=ps_b[:, tb * 16: tb * 16 + c],
                                          lhsT=S["uf"][:, k * SEG + tb * 128: k * SEG + (tb + 1) * 128],
                                          rhs=S["mst"][:, k * 8: k * 8 + c],
                                          start=(k == 0), stop=(k == KC - 1),
                                          skip_group_check=True)
                tensor.drain().then_inc(pe.h, 1)
            EP(_scmm); pe.v += 1
            t_scmm = pe.v

            def _scev(vector, c=c, t_scmm=t_scmm):
                W(vector, (pe, t_scmm))
                for tb in range(2):
                    if c > 0:
                        vector.tensor_scalar(out=S["sc"][:, tb * 16: tb * 16 + c],
                                             in0=ps_b[:, tb * 16: tb * 16 + c],
                                             scalar1=1.0 / 8192.0, scalar2=None, op0=ALU.mult)
                    if c < 8:
                        vector.memset(S["sc"][:, tb * 16 + c: tb * 16 + 8], NEG)
                    vector.tensor_scalar(out=S["sc"][:, tb * 16 + 8: tb * 16 + 9],
                                         in0=ps_b[:, tb * 16 + 8: tb * 16 + 9],
                                         scalar1=S["vsc"][:, tb: tb + 1], scalar2=None, op0=ALU.mult)
                vector.drain()
                for tb in range(2):
                    vector.tensor_reduce(out=S["vsc"][:, 26 + tb: 27 + tb],
                                         in_=S["sc"][:, tb * 16: tb * 16 + 9],
                                         axis=AX.X, op=ALU.max, negate=True)
                vector.drain().then_inc(dv.h, 1)
            EV(_scev); dv.v += 1
            t_scev = dv.v
            psfree["b"] = (dv, t_scev)

            def _scexp(scalar, t_scev=t_scev):
                W(scalar, (dv, t_scev))
                for tb in range(2):
                    scalar.activation(out=S["aw"][:, tb * 16: tb * 16 + 9],
                                      in_=S["sc"][:, tb * 16: tb * 16 + 9], func=AF.Exp,
                                      bias=S["vsc"][:, 26 + tb: 27 + tb],
                                      accum_out=S["vsc"][:, 28 + tb: 29 + tb])
                scalar.drain().then_inc(ac.h, 1)
            EA(_scexp); ac.v += 1
            t_sce = ac.v

            def _scn(vector, t_sce=t_sce):
                W(vector, (ac, t_sce))
                for tb in range(2):
                    vector.reciprocal(out=S["vsc"][:, 30 + tb: 31 + tb],
                                      in_=S["vsc"][:, 28 + tb: 29 + tb])
                vector.drain()
                for tb in range(2):
                    vector.tensor_scalar(out=S["aw"][:, tb * 16: tb * 16 + 9],
                                         in0=S["aw"][:, tb * 16: tb * 16 + 9],
                                         scalar1=S["vsc"][:, 30 + tb: 31 + tb],
                                         scalar2=None, op0=ALU.mult)
                vector.drain()
                for tb in range(2):
                    vector.tensor_copy(out=st(tb, 6), in_=S["aw"][:, tb * 16 + 8: tb * 16 + 9])
                vector.drain().then_inc(dv.h, 1)
            EV(_scn); dv.v += 1
            t_aw = dv.v

            t_gt_prev = tk.get("awp_free", 0)
            for tb in range(2):
                def _gtp0(vector, tb=tb, t_aw=t_aw, wpe=t_gt_prev):
                    W(vector, (dv, t_aw))
                    if wpe:
                        vector.wait_ge(pe.h, wpe)
                    vector.tensor_copy(out=S["awp"][:, 0:9], in_=S["aw"][:, tb * 16: tb * 16 + 9])
                    vector.drain().then_inc(dv.h, 1)
                EV(_gtp0); dv.v += 1
                t_awp = dv.v

                def _gtp(tensor, t_awp=t_awp, bw=psfree["a"]):
                    W(tensor, (dv, t_awp))
                    if bw is not None:
                        W(tensor, bw)
                    tensor.transpose(out=ps_a[0:16, 0:128], in_=S["awp"][:, 0:16],
                                     identity=S["ident"][:])
                    tensor.drain().then_inc(pe.h, 1)
                EP(_gtp); pe.v += 1
                t_gt = pe.v
                t_gt_prev = t_gt
                tk["awp_free"] = t_gt

                def _gte(vector, tb=tb, t_gt=t_gt, wg=tk.get("gwr", 0)):
                    W(vector, (pe, t_gt))
                    if wg:
                        vector.wait_ge(gw.h, wg)
                    vector.tensor_copy(out=S["gcm"][0:16, tb * 128: (tb + 1) * 128],
                                       in_=ps_a[0:16, 0:128])
                    vector.drain().then_inc(dv.h, 1)
                EV(_gte); dv.v += 1
                psfree["a"] = (dv, dv.v)
            t_gcm = dv.v

            def _gwr(gpsimd, tok0=tok0, t_gcm=t_gcm):
                W(gpsimd, (dv, t_gcm))
                gpsimd.dma_start(out=scr_g[:, tok0: tok0 + SEG],
                                 in_=S["gcm"][0:8, 0:SEG]).then_inc(gw.h, 16)
            EG(_gwr); gw.v += 16
            t_scrg = gw.v
            tk["gwr"] = t_scrg

            if ph < 4:
                continue
            for bl in range(2):
                slot = si * 2 + bl

                def _smm(tensor, bl=bl, slot=slot, t_pktm=t_pktm, t_vtm=t_vtm,
                         sw=psfree["scum"]):
                    W(tensor, (ac, t_pktm), (dv, t_vtm))
                    if sw is not None:
                        W(tensor, sw)
                    for h in range(H):
                        tensor.matmul(
                            out=ps_scum[0:64, 128 * h: 128 * h + 65],
                            lhsT=S["pktm"][:, bl * 1024 + (h // 2) * 128 + (h % 2) * 64:
                                           bl * 1024 + (h // 2) * 128 + (h % 2) * 64 + 64],
                            rhs=S["vtm"][:, bl * 1040 + 65 * h: bl * 1040 + 65 * h + 65],
                            start=(slot == 0 and h % 4 == 0), stop=(slot == 15),
                            skip_group_check=True)
                    tensor.drain().then_inc(pe.h, 1)
                EP(_smm); pe.v += 1
                t_smm = pe.v

                def _scp(vector, slot=slot, t_smm=t_smm):
                    W(vector, (pe, t_smm))
                    vector.tensor_copy(
                        out=S["stack"][0:64, slot * 1040: (slot + 1) * 1040],
                        in_=ps_scum[0:64, :].rearrange("p (h f) -> p h f", h=16)[:, :, 0:65])
                    vector.drain().then_inc(dv.h, 1)
                EV(_scp); dv.v += 1
                tk["cp"][slot] = dv.v
                psfree["scum"] = (dv, dv.v)

                dch = da if slot % 2 == 0 else db
                def _sdp(gpsimd, slot=slot, w=tk["cp"][slot], dch=dch):
                    W(gpsimd, (dv, w))
                    gpsimd.dma_start(out=S["stack"][64:128, slot * 1040: (slot + 1) * 1040],
                                     in_=S["stack"][0:64, slot * 1040: (slot + 1) * 1040]
                                     ).then_inc(dch.h, 16)
                EG(_sdp); dch.v += 16
                tk["dup"][slot] = (dch, dch.v)

                t_atm_prev = 0
                for g in range(4 if ph >= 41 else 0):
                    if ph == 410 and g > 0:
                        break
                    def _atmm(tensor, bl=bl, g=g, t_pq=t_pq, t_pk=t_pk, bw=psfree["a"]):
                        W(tensor, (dv, max(t_pq, t_pk)))
                        if bw is not None:
                            W(tensor, bw)
                        for j in range(4):
                            h = 4 * g + j
                            rho, par = h // 2, (h % 2) * 64
                            tensor.wait_ge(gp.h, 1)
                            tensor.matmul(
                                out=ps_a[:, j * 128: (j + 1) * 128],
                                lhsT=S["pkf"][par: par + 64, rho * SEG + bl * 128: rho * SEG + bl * 128 + 128],
                                rhs=S["pqf"][par: par + 64, rho * SEG + bl * 128: rho * SEG + bl * 128 + 128],
                                start=True, stop=True)
                        tensor.drain().then_inc(pe.h, 1)
                    EP(_atmm); pe.v += 1
                    t_at = pe.v

                    if ph == 410:
                        continue
                    def _atc(scalar, t_at=t_at, wm=t_atm_prev, t_scrg=t_scrg):
                        W(scalar, (pe, t_at), (gw, t_scrg))
                        if wm:
                            scalar.wait_ge(dv.h, wm)
                        scalar.copy(out=S["atb"][:], in_=ps_a[:])
                        scalar.drain().then_inc(ac.h, 1)
                    EA(_atc); ac.v += 1
                    t_atc = ac.v
                    psfree["a"] = (ac, t_atc)

                    def _atk(vector, t_atc=t_atc):
                        W(vector, (ac, t_atc))
                        vector.tensor_tensor(
                            out=S["atm"][:],
                            in0=S["atb"][:],
                            in1=S["maskA"][:].unsqueeze(1).broadcast_to([128, 4, 128]),
                            op=ALU.mult)
                        vector.drain().then_inc(dv.h, 1)
                    EV(_atk); dv.v += 1
                    t_atm = dv.v
                    t_atm_prev = t_atm

                    if (ph < 42 or ph == 411) and ph != 420:
                        continue
                    def _ii(tensor, bl=bl, g=g, slot=slot, t_atm=t_atm, t_vtm=t_vtm,
                            wdup=(tk["dup"].get(slot - 1, None)),
                            bwo=psfree["o"], bwb=psfree["b"]):
                        W(tensor, (dv, t_atm))
                        if slot > 0 and wdup is not None:
                            W(tensor, wdup)
                        if g == 0:
                            if bwo is not None:
                                W(tensor, bwo)
                            if bwb is not None:
                                W(tensor, bwb)
                        for j in range(4):
                            h = 4 * g + j
                            rho, par = h // 2, (h % 2) * 64
                            bk, off = out_off(h)
                            pout = PS[bk]
                            first = True
                            if slot > 0:
                                tensor.wait_ge(gp.h, 1)
                                tensor.matmul(
                                    out=pout[:, off: off + 65],
                                    lhsT=S["pqf"][par: par + 64,
                                                  rho * SEG + bl * 128: rho * SEG + bl * 128 + 128],
                                    rhs=S["stack"][par: par + 64,
                                                   (slot - 1) * 1040 + 65 * h: (slot - 1) * 1040 + 65 * h + 65],
                                    start=True, stop=False, skip_group_check=True)
                                first = False
                            tensor.wait_ge(gp.h, 1)
                            tensor.matmul(
                                out=pout[:, off: off + 65],
                                lhsT=S["atm"][:, j * 128: (j + 1) * 128],
                                rhs=S["vtm"][:, bl * 1040 + 65 * h: bl * 1040 + 65 * h + 65],
                                start=first, stop=True, skip_group_check=True)
                        tensor.drain().then_inc(pe.h, 1)
                    EP(_ii); pe.v += 1
                t_ii = pe.v

                if ph < 42 or ph in (411, 420):
                    continue
                def _dn(vector, bl=bl, t_ii=t_ii):
                    W(vector, (pe, t_ii))
                    vector.tensor_copy(
                        out=S["dvec"][:, 0:7],
                        in_=ps_o[:, 0:455].rearrange("p (h f) -> p h f", f=65)[:, :, 64:65])
                    vector.tensor_copy(
                        out=S["dvec"][:, 7:14],
                        in_=ps_o[:, 512:967].rearrange("p (h f) -> p h f", f=65)[:, :, 64:65])
                    vector.tensor_copy(
                        out=S["dvec"][:, 14:16],
                        in_=ps_b[:, 0:130].rearrange("p (h f) -> p h f", f=65)[:, :, 64:65])
                    vector.drain()
                    vector.tensor_scalar(out=S["dvec"][:], in0=S["dvec"][:], scalar1=EPS_ATT,
                                         scalar2=None, op0=ALU.add)
                    vector.drain()
                    vector.reciprocal(out=S["rvec"][:], in_=S["dvec"][:])
                    vector.drain()
                    for psu, base, h0, cnt in ((ps_o, 0, 0, 7), (ps_o, 512, 7, 7), (ps_b, 0, 14, 2)):
                        vector.tensor_tensor(
                            out=S["ar"][:].rearrange("p (b h f) -> p b h f", b=2, h=16)[
                                :, bl, h0: h0 + cnt, :],
                            in0=psu[:, base: base + cnt * 65].rearrange(
                                "p (h f) -> p h f", f=65)[:, :, 0:64],
                            in1=S["rvec"][:, h0: h0 + cnt].unsqueeze(2).broadcast_to(
                                [128, cnt, 64]),
                            op=ALU.mult)
                    vector.drain().then_inc(dv.h, 1)
                EV(_dn); dv.v += 1
                psfree["o"] = (dv, dv.v)
                psfree["b"] = (dv, dv.v)
            t_ar = dv.v

            if ph < 5 or ph in (40, 41, 42, 410, 411, 420):
                continue
            for step in range(4):
                bl, grp = divmod(step, 2)
                bank = "a" if (step % 2 == 0) else "b"
                psu = PS[bank]
                def _tar(tensor, bl=bl, grp=grp, psu=psu, bw=psfree[bank], t_ar=t_ar):
                    W(tensor, (dv, t_ar))
                    if bw is not None:
                        W(tensor, bw)
                    pb = psu[:].bitcast(BF16)
                    for j in range(4):
                        kc = grp * 4 + j
                        tensor.transpose(out=pb[:, j * 128: (j + 1) * 128],
                                         in_=S["ar"][:, bl * 1024 + kc * 128: bl * 1024 + (kc + 1) * 128],
                                         identity=S["identb"][:])
                    tensor.drain().then_inc(pe.h, 1)
                EP(_tar); pe.v += 1
                t_mm = pe.v

                def _ear(scalar, bl=bl, grp=grp, psu=psu, t_mm=t_mm):
                    W(scalar, (pe, t_mm))
                    pb = psu[:].bitcast(BF16)
                    for j in range(4):
                        kc = grp * 4 + j
                        scalar.copy(out=S["arf"][:, kc * SEG + bl * 128: kc * SEG + (bl + 1) * 128],
                                    in_=pb[:, j * 128: (j + 1) * 128])
                    scalar.drain().then_inc(ac.h, 1)
                EA(_ear); ac.v += 1
                psfree[bank] = (ac, ac.v)
            t_arf = ac.v
            t_ph4 = pe.v

            if ph < 6:
                continue
            t_xrl = 0

            t_pgt = {}
            t_hmmn = {}
            if c > 0:
                for n in range(c):
                    nb = n % 2
                    gch = ga if nb == 0 else gb
                    def _gld(gpsimd, n=n, nb=nb, tok0=tok0, t_scrg=t_scrg, gch=gch,
                             wfree=max(t_pgt.get(n - 2, tk.get("gstr_free", 0)), t_vtm)):
                        W(gpsimd, (gw, t_scrg))
                        if wfree:
                            gpsimd.wait_ge(dv.h, wfree)
                        gpsimd.dma_start(out=S["gstr"][:, nb * SEG: (nb + 1) * SEG],
                                         in_=scr_g[n: n + 1, tok0: tok0 + SEG].broadcast_to(
                                             [128, SEG])).then_inc(gch.h, 16)
                    EG(_gld); gch.v += 16
                    t_gld = gch.v

                    def _pgt(vector, n=n, nb=nb, t_gld=t_gld, t_pq=t_pq, gch=gch,
                             wpe=(t_ph4 if n < 2 else t_hmmn[n - 2])):
                        W(vector, (gch, t_gld), (dv, t_pq))
                        vector.wait_ge(pe.h, wpe)
                        for rho in range(KC):
                            vector.tensor_tensor(
                                out=pgt_buf(nb)[:, rho * SEG: (rho + 1) * SEG],
                                in0=S["pqf"][:, rho * SEG: (rho + 1) * SEG],
                                in1=S["gstr"][:, nb * SEG: (nb + 1) * SEG], op=ALU.mult)
                        vector.drain().then_inc(dv.h, 1)
                    EV(_pgt); dv.v += 1
                    t_pgt[n] = dv.v

                    def _hmm(tensor, n=n, nb=nb, c=c, t_p=t_pgt[n],
                             wdup=tk["dup"].get(2 * n + 1, None),
                             bwo=psfree["o"] if n == 0 else None,
                             bwa=psfree["a"] if n == 0 else None,
                             bwb=psfree["b"] if n == 0 else None):
                        W(tensor, (dv, t_p))
                        if wdup is not None:
                            W(tensor, wdup)
                        for bw in (bwo, bwa, bwb):
                            if bw is not None:
                                W(tensor, bw)
                        for h in range(H):
                            rho, par = h // 2, (h % 2) * 64
                            tensor.wait_ge(gp.h, 1)
                            for th in range(2):
                                if th == 0:
                                    o_ap = ps_o[:, 64 * h: 64 * h + 64]
                                else:
                                    o_ap = (ps_a[:, 64 * h: 64 * h + 64] if h < 8
                                            else ps_b[:, 64 * (h - 8): 64 * (h - 8) + 64])
                                tensor.matmul(
                                    out=o_ap,
                                    lhsT=pgt_buf(nb)[par: par + 64,
                                                     rho * SEG + th * 128:
                                                     rho * SEG + th * 128 + 128],
                                    rhs=S["stack"][par: par + 64,
                                                   (2 * n + 1) * 1040 + 65 * h:
                                                   (2 * n + 1) * 1040 + 65 * h + 64],
                                    start=(n == 0 and h % 8 == 0), stop=(n == c - 1),
                                    skip_group_check=True)
                        tensor.drain().then_inc(pe.h, 1)
                    EP(_hmm); pe.v += 1
                    t_hmmn[n] = pe.v
                t_hmm = pe.v
                tk["pgt_free"] = t_hmm
                tk["gstr_free"] = max(t_pgt.values()) if t_pgt else 0

                def _hbe(vector, t_hmm=t_hmm, t_scrg=t_scrg):
                    W(vector, (pe, t_hmm), (gw, t_scrg))
                    vector.tensor_copy(out=S["hb"][:, 0:512], in_=ps_a[:])
                    vector.tensor_copy(out=S["hb"][:, 512:1024], in_=ps_b[:])
                    vector.drain().then_inc(dv.h, 1)
                EV(_hbe); dv.v += 1
                psfree["a"] = (dv, dv.v)
                psfree["b"] = (dv, dv.v)
                t_hb = dv.v
            else:
                t_hb = 0

            for tb in range(2):
                for half in range(2):
                    bank = "a" if (half == 0) else "b"
                    psu = PS[bank]
                    def _apm(tensor, tb=tb, half=half, psu=psu, bw=psfree[bank],
                             t_arf=t_arf, t_wq_ready=t_wq_ready):
                        W(tensor, (ac, t_arf), (wt, t_wq_ready))
                        if bw is not None:
                            W(tensor, bw)
                        for k in range(KC):
                            tensor.matmul(out=psu[:],
                                          lhsT=S["arf"][:, k * SEG + tb * 128: k * SEG + (tb + 1) * 128],
                                          rhs=S["wo"][:, k * D + half * 512: k * D + half * 512 + 512],
                                          start=(k == 0), stop=(k == KC - 1))
                        tensor.drain().then_inc(pe.h, 1)
                    EP(_apm); pe.v += 1
                    t_mm = pe.v

                    def _ape(scalar, tb=tb, half=half, psu=psu, t_mm=t_mm,
                             wso=tk.get("ep_a_free", 0)):
                        W(scalar, (pe, t_mm))
                        if wso:
                            scalar.wait_ge(so.h, wso)
                        scalar.activation(out=S["ep_a"][:, half * 512: half * 512 + 512],
                                          in_=psu[:], func=AF.Identity, scale=st(tb, 6))
                        scalar.drain().then_inc(ac.h, 1)
                    EA(_ape); ac.v += 1
                    psfree[bank] = (ac, ac.v)
                t_apa = ac.v

                for half in range(2):
                    bank = "a" if (half == 0) else "b"
                    psu = PS[bank]
                    def _ffm(tensor, tb=tb, half=half, psu=psu, bw=psfree[bank],
                             t_xqf=t_xqf):
                        W(tensor, (dv, t_xqf))
                        if bw is not None:
                            W(tensor, bw)
                        for k in range(KC):
                            tensor.matmul(out=psu[:],
                                          lhsT=S["xqf"][:, k * SEG + tb * 128: k * SEG + (tb + 1) * 128],
                                          rhs=S["wf"][:, k * D + half * 512: k * D + half * 512 + 512],
                                          start=(k == 0), stop=(k == KC - 1))
                        tensor.drain().then_inc(pe.h, 1)
                    EP(_ffm); pe.v += 1
                    t_mm = pe.v

                    def _ffe(vector, tb=tb, half=half, psu=psu, t_mm=t_mm,
                             wsl=tk.get("ep_f_free", 0)):
                        W(vector, (pe, t_mm), (bc, t_wq2))
                        if wsl:
                            vector.wait_ge(ac.h, wsl)
                        vector.tensor_tensor(out=S["ep_f"][:, half * 512: half * 512 + 512],
                                             in0=psu[:], in1=S["wsb"][:, half * 512: half * 512 + 512],
                                             op=ALU.mult)
                        vector.drain().then_inc(dv.h, 1)
                    EV(_ffe); dv.v += 1
                    psfree[bank] = (dv, dv.v)
                t_f1 = dv.v

                def _f2(vector, tb=tb, t_f1=t_f1):
                    vector.drain()
                    vector.scalar_tensor_tensor(out=S["ep_f"][:], in0=S["ep_f"][:],
                                                scalar=st(tb, 4), in1=S["bbc"][:],
                                                op0=ALU.mult, op1=ALU.add)
                    vector.drain().then_inc(dv.h, 1)
                EV(_f2); dv.v += 1
                t_f2 = dv.v

                def _sil(scalar, t_f2=t_f2):
                    W(scalar, (dv, t_f2))
                    scalar.activation(out=S["ep_g"][:], in_=S["ep_f"][:], func=AF.Exp,
                                      scale=-1.0)
                    scalar.drain().then_inc(ac.h, 1)
                EA(_sil); ac.v += 1
                t_sile = ac.v
                tk["ep_f_free"] = t_sile
                tk["epf_free_ac"] = t_sile

                def _sil2(vector, t_sile=t_sile):
                    W(vector, (ac, t_sile))
                    vector.tensor_scalar(out=S["ep_g"][:], in0=S["ep_g"][:],
                                         scalar1=1.0, scalar2=None, op0=ALU.add)
                    vector.drain()
                    vector.reciprocal(out=S["ep_g"][:], in_=S["ep_g"][:])
                    vector.drain()
                    vector.tensor_tensor(out=S["ep_g"][:], in0=S["ep_g"][:],
                                         in1=S["ep_f"][:], op=ALU.mult)
                    vector.drain().then_inc(dv.h, 1)
                EV(_sil2); dv.v += 1
                t_sil = dv.v

                def _gg(vector, tb=tb, t_aw=t_aw, ws1=tk.get("s1_done", 0)):
                    if ws1:
                        vector.wait_ge(ac.h, ws1)
                    vector.tensor_scalar(out=S["vsc"][:, 24:25], in0=st(tb, 6),
                                         scalar1=st(tb, 1), scalar2=None, op0=ALU.mult)
                    vector.tensor_scalar(out=S["vsc"][:, 25:26], in0=st(tb, 6),
                                         scalar1=st(tb, 5), scalar2=None, op0=ALU.mult)
                    vector.drain().then_inc(dv.h, 1)
                EV(_gg); dv.v += 1
                t_gg = dv.v

                def _s1(scalar, tb=tb, t_gg=t_gg):
                    W(scalar, (dv, t_gg))
                    scalar.activation(out=S["ep_e"][:], in_=S["xsb"][:, tb * D: (tb + 1) * D],
                                      func=AF.Identity, scale=S["vsc"][:, 24:25],
                                      bias=S["vsc"][:, 25:26])
                    scalar.drain().then_inc(ac.h, 1)
                EA(_s1); ac.v += 1
                t_s1 = ac.v
                tk["s1_done"] = t_s1
                tk["xsb_free_ac"] = t_s1

                def _epi(vector, tb=tb, c=c, t_s1=t_s1, t_apa=t_apa, t_sil=t_sil, t_hb=t_hb):
                    W(vector, (ac, max(t_s1, t_apa)))
                    vector.tensor_tensor(out=S["ep_e"][:], in0=S["ep_e"][:], in1=S["ep_a"][:],
                                         op=ALU.add)
                    vector.drain()
                    if c > 0:
                        if tb == 0:
                            vector.tensor_tensor(out=S["ep_e"][:], in0=S["ep_e"][:],
                                                 in1=ps_o[:], op=ALU.add)
                        else:
                            vector.tensor_tensor(out=S["ep_e"][:], in0=S["ep_e"][:],
                                                 in1=S["hb"][:], op=ALU.add)
                        vector.drain()
                    vector.tensor_tensor(out=S["ep_f"][:], in0=S["ep_e"][:], in1=S["ep_g"][:],
                                         op=ALU.mult)
                    vector.drain()
                    vector.tensor_tensor(out=S["ep_a"][:], in0=S["xsb"][:, tb * D: (tb + 1) * D],
                                         in1=S["ep_f"][:], op=ALU.add)
                    vector.drain().then_inc(dv.h, 1)
                EV(_epi); dv.v += 1
                t_epi = dv.v
                tk["xsb_free_dv"] = t_epi
                if tb == 0 and c > 0:
                    psfree["o"] = (dv, t_epi)

                def _out(gpsimd, b=b, tok0=tok0, tb=tb, t_epi=t_epi):
                    W(gpsimd, (dv, t_epi))
                    gpsimd.dma_start(out=out_d[b, tok0 + tb * 128: tok0 + (tb + 1) * 128, :],
                                     in_=S["ep_a"][:]).then_inc(so.h, 16)
                EG(_out); so.v += 16
                tk["ep_a_free"] = so.v

    if dbg_d:
        t_all = (dv.v, ac.v, pe.v)
        def _dbg(gpsimd, t_all=t_all):
            gpsimd.wait_ge(dv.h, t_all[0])
            gpsimd.wait_ge(ac.h, t_all[1])
            gpsimd.wait_ge(pe.h, t_all[2])
            for nm, src in dbg_done:
                gpsimd.dma_start(out=dbg_d[nm], in_=src).then_inc(so.h, 16)
        EG(_dbg)
        so.v += 16 * len(dbg_done)

    if so.v == 0:
        def _dummy(gpsimd):
            gpsimd.wait_ge(bc.h, 32)
            gpsimd.dma_start(out=out_d[0, 0:128, :], in_=S["bbc"][:]).then_inc(so.h, 16)
        EG(_dummy); so.v += 16
    fin = so.v
    def _fin(gpsimd, fin=fin):
        gpsimd.wait_ge(so.h, fin)
    EG(_fin)

    blk.__exit__(None, None, None)
    return nc


_NC_CACHE = {}


def kernel(L0_attn_x, L0_attn_norm_w, L0_attn_Wq, L0_attn_Wk, L0_attn_Wv,
           L0_attn_Wo, L0_attn_Wu, L0_ffn_weight, L0_ffn_bias):
    x = np.ascontiguousarray(np.asarray(L0_attn_x, dtype=np.float32))
    wq = np.ascontiguousarray(np.asarray(L0_attn_Wq, dtype=np.float32).T)
    wk = np.ascontiguousarray(np.asarray(L0_attn_Wk, dtype=np.float32).T)
    wv = np.ascontiguousarray(np.asarray(L0_attn_Wv, dtype=np.float32).T)
    wo = np.ascontiguousarray(np.asarray(L0_attn_Wo, dtype=np.float32).T)
    wu = np.ascontiguousarray(np.asarray(L0_attn_Wu, dtype=np.float32).T)
    wf = np.ascontiguousarray(np.asarray(L0_ffn_weight, dtype=np.float32).T)
    wfn = np.ascontiguousarray(np.asarray(L0_ffn_weight, dtype=np.float32))
    nw = np.ascontiguousarray(np.asarray(L0_attn_norm_w, dtype=np.float32))
    fb = np.ascontiguousarray(np.asarray(L0_ffn_bias, dtype=np.float32))

    if "nc" not in _NC_CACHE:
        _NC_CACHE["nc"] = build()
    nc = _NC_CACHE["nc"]

    in_maps = []
    for core in range(8):
        in_maps.append({
            "x": np.ascontiguousarray(x[2 * core: 2 * core + 2]),
            "wqt": wq, "wkt": wk, "wvt": wv, "wot": wo, "wut": wu,
            "wft": wf, "wfn": wfn, "nw": nw, "fb": fb,
        })
    res = run_bass_kernel_spmd(nc, in_maps, core_ids=list(range(8)))
    out = np.concatenate([np.asarray(r["out"]) for r in res.results], axis=0)
    return out.astype(np.float32)


# version 07
# speedup vs baseline: 1.2991x; 1.2991x over previous
import numpy as np
import concourse.bass as bass
import concourse.mybir as mybir
from concourse.bass_utils import run_bass_kernel_spmd

F32 = mybir.dt.float32
BF16 = mybir.dt.bfloat16
AF = mybir.ActivationFunctionType
ALU = mybir.AluOpType
AX = mybir.AxisListType

BC, T, D, H, DK, SEG = 2, 2048, 1024, 16, 64, 256
NSB, KC = T // SEG, D // 128
EPS_ATT, EPS_LN, EPS_FFN = 1e-6, 1e-5, 1e-8
NEG = -30000.0


def out_off(h):
    if h < 7:
        return ("o", 65 * h)
    if h < 14:
        return ("o", 512 + 65 * (h - 7))
    return ("b", 65 * (h - 14))


def build(dbg_specs=(), n_units=16, ph=99):
    nc = bass.Bass("TRN2", target_bir_lowering=False)

    x_d = nc.dram_tensor("x", [BC, T, D], F32, kind="ExternalInput").ap()
    w_d = {nm: nc.dram_tensor(nm, [D, D], F32, kind="ExternalInput").ap()
           for nm in ("wqt", "wkt", "wvt", "wot", "wut", "wft", "wfn")}
    nw_d = nc.dram_tensor("nw", [D], F32, kind="ExternalInput").ap()
    fb_d = nc.dram_tensor("fb", [D], F32, kind="ExternalInput").ap()
    out_d = nc.dram_tensor("out", [BC, T, D], F32, kind="ExternalOutput").ap()
    scr_g = nc.dram_tensor("scr_g", [8, T], F32, kind="Internal").ap()
    scr_ws = nc.dram_tensor("scr_ws", [2, D], F32, kind="Internal").ap()
    dbg_d = {nm: nc.dram_tensor("dbg_" + nm, list(shape), dt, kind="ExternalOutput").ap()
             for nm, shape, dt in dbg_specs}
    dbg_done = []

    S = {}
    def al(name, shape, dtype=F32):
        S[name] = nc.alloc_sbuf_tensor(name, list(shape), dtype)

    for w in ("wq", "wk", "wv", "wu", "wo", "wf"):
        al(w, [128, KC * D], BF16)
    al("stack", [128, 16 * H * 65], BF16)
    for t in ("xnf", "xn2f", "xqf", "pqf", "pkf", "uf", "pktm"):
        al(t, [128, KC * SEG], BF16)
    al("vtm", [128, 2 * H * 65], BF16)
    al("xsb", [128, 2 * D], F32)
    al("xn_t", [128, D], F32)
    al("zone", [128, 6144], F32)
    Z = S["zone"]
    S["ystrip"] = Z[:, 0:2048]
    S["sd"] = Z[:, 2048:4096]
    S["d0"] = Z[:, 4096:5120]
    S["d1"] = Z[:, 5120:6144]
    S["ep_a"] = Z[:, 0:1024]
    S["ep_e"] = Z[:, 1024:2048]
    S["ep_f"] = Z[:, 2048:3072]
    S["ep_g"] = Z[:, 3072:4096]
    S["hb"] = Z[:, 4096:5120]
    S["elu1"] = Z[:, 5120:5632]
    S["elu2"] = Z[:, 5632:6144]
    S["rwsb"] = Z[:, 5120:6144]
    S["arf"] = S["xn2f"]
    S["ar"] = S["uf"][:]
    S["atb"] = Z[:, 4096:4352].bitcast(BF16)
    S["atm"] = Z[:, 4352:4608].bitcast(BF16)
    S["s2m"] = Z[:, 4608:4800].bitcast(BF16)
    S["gcm"] = bass.AP(Z[:, 4800:5056].tensor, Z[:, 4800:5056].offset,
                       [[1, 16], [1, 256]]) if False else Z[0:16, 4800:5056]
    S["gstr"] = Z[:, 5888:6144].bitcast(BF16)
    S["iotc"] = Z[:, 1024:1152]
    al("iotp", [128, 1], F32)
    al("ident", [128, 128], F32)
    al("identb", [128, 128], BF16)
    al("diag", [128, 128], F32)
    al("maskA", [128, 128], BF16)
    al("stats", [128, 16], F32)
    al("mst", [128, KC * 8], BF16)
    al("aw", [128, 32], F32)
    al("awp", [128, 16], F32)
    al("nwv", [128, KC], F32)
    al("bbc", [128, D], F32)
    al("wsb", [128, D], F32)
    al("dvec", [128, 16], F32)
    al("rvec", [128, 16], F32)
    al("vsc", [128, 32], F32)
    al("sc", [128, 32], F32)
    al("wsv", [128, KC], F32)
    al("cb", [128, 4], F32)

    def pgt_buf(nb):
        return S["uf"] if nb == 0 else S["xnf"]

    ps_scum = nc.alloc_psum_tensor("ps_scum", [64, 2048], F32)
    ps_a = nc.alloc_psum_tensor("ps_a", [128, 512], F32)
    ps_b = nc.alloc_psum_tensor("ps_b", [128, 512], F32)
    ps_o = nc.alloc_psum_tensor("ps_o", [128, 1024], F32)
    PS = {"a": ps_a, "b": ps_b, "o": ps_o}

    class Ctr:
        def __init__(self, name):
            self.h = nc.semaphore(name).__enter__()
            self.v = 0
    (ld, gd, pe, dv, ac, gp, so, xl, wa, wb, bc, wt, gw, da, db, ga, gb) = (
        Ctr(n) for n in ("ld", "gd", "pe", "dv", "ac", "gp", "so", "xl", "wa",
                         "wb", "bc", "wt", "gw", "da", "db", "ga", "gb"))

    blk = nc.Block().__enter__()
    _PEND = []
    def EV(fn): _PEND.append(("vector", fn))
    def EA(fn): _PEND.append(("scalar", fn))
    def EP(fn): _PEND.append(("tensor", fn))
    def EG(fn): _PEND.append(("gpsimd", fn))
    def ES(fn): _PEND.append(("sync", fn))

    def W(e, *pairs):
        for ctr, val in pairs:
            if val > 0:
                e.wait_ge(ctr.h, val)

    psfree = {"a": None, "b": None, "o": None, "scum": None}
    def bank_wait(e, *banks):
        for bk in banks:
            if psfree[bk] is not None:
                W(e, psfree[bk])

    def _s0(sync):
        sync.dma_start(out=S["nwv"][:], in_=nw_d.rearrange("(c p) -> p c", p=128),
                       allow_slow_non_contiguous=True).then_inc(bc.h, 16)
        sync.dma_start(out=S["bbc"][:], in_=fb_d.unsqueeze(0).broadcast_to([128, D])).then_inc(bc.h, 16)
    ES(_s0); bc.v += 32

    def _g0(gpsimd):
        for w, dn in (("wq", "wqt"), ("wk", "wkt"), ("wv", "wvt"), ("wu", "wut"), ("wo", "wot")):
            gpsimd.dma_start(out=S[w][:].rearrange("p (c d) -> p c d", d=D),
                             in_=w_d[dn].rearrange("(c p) d -> p c d", p=128)).then_inc(wt.h, 16)
        gpsimd.iota(S["iotc"][:], pattern=[[1, 128]], base=0, channel_multiplier=0,
                    allow_small_or_imprecise_dtypes=True)
        gpsimd.iota(S["iotp"][:], pattern=[[0, 1]], base=0, channel_multiplier=1,
                    allow_small_or_imprecise_dtypes=True)
        gpsimd.drain().then_inc(gp.h, 1)
    EG(_g0); wt.v += 80; gp.v += 1
    t_wload, t_iota = wt.v, gp.v

    def _v0(vector):
        vector.wait_ge(gp.h, t_iota)
        vector.tensor_scalar(out=S["ident"][:], in0=S["iotc"][:], scalar1=S["iotp"][:],
                             scalar2=None, op0=ALU.subtract)
        vector.drain()
        vector.tensor_scalar(out=S["maskA"][:], in0=S["ident"][:], scalar1=0.0,
                             scalar2=None, op0=ALU.is_ge)
        vector.tensor_scalar(out=S["ident"][:], in0=S["ident"][:], scalar1=0.0,
                             scalar2=None, op0=ALU.is_equal)
        vector.drain()
        vector.tensor_copy(out=S["identb"][:], in_=S["ident"][:])
        vector.tensor_scalar(out=S["vsc"][:, 2:3], in0=S["iotp"][:], scalar1=1.0,
                             scalar2=32.0, op0=ALU.add, op1=ALU.mult)
        vector.tensor_scalar(out=S["vsc"][:, 3:4], in0=S["iotp"][:], scalar1=129.0,
                             scalar2=32.0, op0=ALU.add, op1=ALU.mult)
        vector.drain()
        vector.reciprocal(out=S["vsc"][:, 0:1], in_=S["vsc"][:, 2:3])
        vector.reciprocal(out=S["vsc"][:, 1:2], in_=S["vsc"][:, 3:4])
        for blki in range(2):
            for h in range(H):
                vector.memset(S["vtm"][:, blki * 1040 + 65 * h + 64: blki * 1040 + 65 * h + 65], 1.0)
        vector.memset(S["awp"][:], 0.0)
        vector.memset(S["cb"][:, 0:1], EPS_LN)
        vector.memset(S["cb"][:, 1:2], EPS_ATT)
        vector.memset(S["cb"][:, 2:3], 128.5)
        vector.drain().then_inc(dv.h, 1)
    EV(_v0); dv.v += 1
    t_consts = dv.v

    t_rd = {}
    for cc in range(KC):
        half = cc % 2
        wch = wa if half == 0 else wb
        def _wl(sync, cc=cc, half=half, wch=wch):
            if cc >= 2:
                sync.wait_ge(dv.h, t_rd[cc - 2])
            sync.dma_start(out=S["xsb"][:, half * D: (half + 1) * D],
                           in_=w_d["wfn"][cc * 128: (cc + 1) * 128, :]).then_inc(wch.h, 16)
        ES(_wl); wch.v += 16
        t_wl = wch.v

        def _wr(vector, cc=cc, half=half, t_wl=t_wl, wch=wch):
            vector.wait_ge(wch.h, t_wl)
            vector.tensor_reduce(out=S["wsv"][:, cc: cc + 1],
                                 in_=S["xsb"][:, half * D: (half + 1) * D],
                                 axis=AX.X, op=ALU.max, apply_absolute_value=True)
            vector.drain().then_inc(dv.h, 1)
        EV(_wr); dv.v += 1
        t_rd[cc] = dv.v

    def _wq1(vector):
        vector.drain()
        vector.tensor_scalar(out=S["wsv"][:], in0=S["wsv"][:], scalar1=EPS_FFN,
                             scalar2=1.0 / 127.0, op0=ALU.max, op1=ALU.mult)
        vector.drain()
        vector.reciprocal(out=S["vsc"][:, 16:24], in_=S["wsv"][:])
        vector.drain().then_inc(dv.h, 1)
    EV(_wq1); dv.v += 1
    t_wq1 = dv.v

    def _wq2(sync, _wbc=bc.v + 32):
        sync.wait_ge(dv.h, t_wq1)
        sync.dma_start(out=scr_ws[1:2, :].rearrange("o (c p) -> p (o c)", p=128),
                       in_=S["wsv"][:], allow_slow_non_contiguous=True).then_inc(bc.h, 16)
        sync.dma_start(out=scr_ws[0:1, :].rearrange("o (c p) -> p (o c)", p=128),
                       in_=S["vsc"][:, 16:24], allow_slow_non_contiguous=True).then_inc(bc.h, 16)
        sync.wait_ge(bc.h, _wbc)
        sync.dma_start(out=S["wsb"][:], in_=scr_ws[1:2, :].broadcast_to([128, D])).then_inc(bc.h, 16)
        sync.dma_start(out=S["rwsb"][:], in_=scr_ws[0:1, :].broadcast_to([128, D])).then_inc(bc.h, 16)
    ES(_wq2); bc.v += 64
    t_wq2 = bc.v

    t_wq3c = {}
    for cc in range(KC):
        half = cc % 2
        wch = wa if half == 0 else wb
        def _wl2(sync, cc=cc, half=half, wch=wch):
            if cc >= 2:
                sync.wait_ge(dv.h, t_wq3c[cc - 2])
            sync.dma_start(out=S["xsb"][:, half * D: (half + 1) * D],
                           in_=w_d["wft"][cc * 128: (cc + 1) * 128, :]).then_inc(wch.h, 16)
        ES(_wl2); wch.v += 16
        t_wl2 = wch.v

        def _wq3(vector, cc=cc, half=half, t_wl2=t_wl2, t_wq2=t_wq2, wch=wch):
            vector.wait_ge(wch.h, t_wl2)
            vector.wait_ge(bc.h, t_wq2)
            src = S["xsb"][:, half * D: (half + 1) * D]
            y, r, d_ = S["ystrip"][:, 0:D], S["sd"][:, 0:D], S["d0"][:]
            vector.tensor_tensor(out=y, in0=src, in1=S["rwsb"][:], op=ALU.mult)
            vector.drain()
            vector.tensor_scalar(out=r, in0=y, scalar1=128.5, scalar2=None, op0=ALU.add)
            vector.drain()
            for k in range(7, -1, -1):
                sv = float(1 << k)
                vector.tensor_scalar(out=d_, in0=r, scalar1=sv, scalar2=-sv,
                                     op0=ALU.is_ge, op1=ALU.mult)
                vector.drain()
                vector.tensor_tensor(out=r, in0=r, in1=d_, op=ALU.add)
                vector.drain()
            vector.scalar_tensor_tensor(out=S["wf"][:, cc * D: (cc + 1) * D],
                                        in0=y, scalar=0.5, in1=r, op0=ALU.add, op1=ALU.subtract)
            vector.drain().then_inc(dv.h, 1)
        EV(_wq3); dv.v += 1
        t_wq3c[cc] = dv.v
    tk0_wq3 = dv.v

    tk = {"xsb_free_ac": 0, "xsb_free_dv": 0, "xn_t_free_pe": 0, "xn_t_free_dv": 0,
          "xrl_free": 0, "epf_free_ac": 0, "strip_free": 0,
          "dup": {}, "cp": {}}

    def st(tb, j):
        return S["stats"][:, tb * 8 + j: tb * 8 + j + 1]

    units = [(b, si) for b in range(BC) for si in range(NSB)][:n_units]
    for b, si in units:
        if True:
            c = si
            tok0 = si * SEG
            pe_prev = pe.v

            def _ph0(sync, b=b, tok0=tok0, w1=tk["xsb_free_ac"],
                     w2=max(tk["xsb_free_dv"], tk0_wq3 if (b == 0 and si == 0) else 0)):
                if w1:
                    sync.wait_ge(ac.h, w1)
                if w2:
                    sync.wait_ge(dv.h, w2)
                sync.dma_start(
                    out=S["xsb"][:].rearrange("p (tb d) -> p tb d", d=D),
                    in_=x_d[b, tok0: tok0 + SEG, :].rearrange("(tb p) d -> p tb d", p=128),
                ).then_inc(xl.h, 16)
            ES(_ph0); xl.v += 16
            t_x = xl.v

            def _p1v(vector, t_x=t_x):
                W(vector, (xl, t_x), (dv, t_consts), (bc, t_wq2))
                for tb in range(2):
                    vector.tensor_reduce(out=S["vsc"][:, 4 + tb: 5 + tb],
                                         in_=S["xsb"][:, tb * D: (tb + 1) * D], axis=AX.X, op=ALU.add)
                vector.drain()
                for tb in range(2):
                    vector.tensor_scalar(out=st(tb, 0), in0=S["vsc"][:, 4 + tb: 5 + tb],
                                         scalar1=-1.0 / D, scalar2=None, op0=ALU.mult)
                vector.drain().then_inc(dv.h, 1)
            EV(_p1v); dv.v += 1
            t_mean = dv.v

            def _p1a(scalar, t_x=t_x, t_mean=t_mean, wdv=tk["strip_free"]):
                W(scalar, (xl, t_x), (dv, t_mean))
                if wdv:
                    scalar.wait_ge(dv.h, wdv)
                for tb in range(2):
                    scalar.activation(out=S["ep_f" if tb == 0 else "ep_g"][:],
                                      in_=S["xsb"][:, tb * D: (tb + 1) * D],
                                      func=AF.Square, accum_out=S["vsc"][:, 6 + tb: 7 + tb])
                scalar.drain().then_inc(ac.h, 1)
            EA(_p1a); ac.v += 1
            t_sq = ac.v

            def _p1v2(vector, t_sq=t_sq):
                W(vector, (ac, t_sq))
                for tb in range(2):
                    vector.tensor_scalar(out=S["vsc"][:, 8 + tb: 9 + tb], in0=st(tb, 0),
                                         scalar1=st(tb, 0), scalar2=None, op0=ALU.mult)
                    vector.tensor_scalar(out=S["vsc"][:, 10 + tb: 11 + tb],
                                         in0=S["vsc"][:, 6 + tb: 7 + tb],
                                         scalar1=1.0 / D, scalar2=None, op0=ALU.mult)
                vector.drain()
                for tb in range(2):
                    vector.tensor_tensor(out=S["vsc"][:, 12 + tb: 13 + tb],
                                         in0=S["vsc"][:, 10 + tb: 11 + tb],
                                         in1=S["vsc"][:, 8 + tb: 9 + tb], op=ALU.subtract)
                vector.drain().then_inc(dv.h, 1)
            EV(_p1v2); dv.v += 1
            t_var = dv.v

            def _p1a2(scalar, t_var=t_var):
                W(scalar, (dv, t_var))
                for tb in range(2):
                    scalar.activation(out=S["vsc"][:, 14 + tb: 15 + tb],
                                      in_=S["vsc"][:, 12 + tb: 13 + tb], func=AF.Ln, bias=S["cb"][:, 0:1])
                scalar.drain()
                for tb in range(2):
                    scalar.activation(out=st(tb, 1), in_=S["vsc"][:, 14 + tb: 15 + tb],
                                      func=AF.Exp, scale=-0.5)
                scalar.drain().then_inc(ac.h, 1)
            EA(_p1a2); ac.v += 1
            t_sd_ = ac.v

            def _p1v3(vector, t_sd_=t_sd_):
                W(vector, (ac, t_sd_))
                for tb in range(2):
                    vector.tensor_scalar(out=st(tb, 5), in0=st(tb, 0), scalar1=st(tb, 1),
                                         scalar2=None, op0=ALU.mult)
                vector.drain().then_inc(dv.h, 1)
            EV(_p1v3); dv.v += 1
            t_istd = dv.v

            def _p1v3b(vector, t_istd=t_istd):
                for tb in range(2):
                    vector.tensor_scalar(out=S["vsc"][:, 18 + tb: 19 + tb],
                                         in0=S["vsc"][:, 12 + tb: 13 + tb],
                                         scalar1=st(tb, 1), scalar2=st(tb, 1),
                                         op0=ALU.mult, op1=ALU.mult)
                vector.drain().then_inc(dv.h, 1)
            EV(_p1v3b); dv.v += 1
            t_rin = dv.v

            def _p1a4(scalar, t_rin=t_rin):
                W(scalar, (dv, t_rin))
                for tb in range(2):
                    scalar.activation(out=S["vsc"][:, 20 + tb: 21 + tb],
                                      in_=S["vsc"][:, 18 + tb: 19 + tb], func=AF.Ln, bias=S["cb"][:, 1:2])
                scalar.drain()
                for tb in range(2):
                    scalar.activation(out=st(tb, 2), in_=S["vsc"][:, 20 + tb: 21 + tb],
                                      func=AF.Exp, scale=-0.5)
                scalar.drain().then_inc(ac.h, 1)
            EA(_p1a4); ac.v += 1
            t_rmsq = ac.v

            t_xqf_prev = tk["strip_free"]
            t_xn = {}
            for tb in range(2):
                def _xn(scalar, tb=tb, t_istd=t_istd,
                        wpe=tk["xn_t_free_pe"], wdv=tk["xn_t_free_dv"]):
                    W(scalar, (dv, t_istd))
                    if wpe:
                        scalar.wait_ge(pe.h, wpe)
                    if wdv:
                        scalar.wait_ge(dv.h, wdv)
                    scalar.activation(out=S["xn_t"][:], in_=S["xsb"][:, tb * D: (tb + 1) * D],
                                      func=AF.Identity, bias=st(tb, 5), scale=st(tb, 1))
                    scalar.drain().then_inc(ac.h, 1)
                EA(_xn); ac.v += 1
                t_xn[tb] = ac.v

                def _am(vector, tb=tb, t_x2=t_xn[tb]):
                    W(vector, (ac, t_x2))
                    vector.tensor_reduce(out=S["vsc"][:, 16 + tb: 17 + tb], in_=S["xn_t"][:],
                                         axis=AX.X, op=ALU.max, apply_absolute_value=True)
                    vector.drain()
                    vector.tensor_scalar(out=st(tb, 4), in0=S["vsc"][:, 16 + tb: 17 + tb],
                                         scalar1=EPS_FFN, scalar2=1.0 / 127.0,
                                         op0=ALU.max, op1=ALU.mult)
                    vector.drain()
                    vector.reciprocal(out=st(tb, 3), in_=st(tb, 4))
                    vector.drain()
                    vector.tensor_scalar(out=S["vsc"][:, 8:9], in0=st(tb, 1),
                                         scalar1=st(tb, 2), scalar2=None, op0=ALU.mult)
                    vector.tensor_scalar(out=S["vsc"][:, 9:10], in0=st(tb, 5),
                                         scalar1=st(tb, 2), scalar2=None, op0=ALU.mult)
                    vector.tensor_scalar(out=S["vsc"][:, 10:11], in0=st(tb, 1),
                                         scalar1=st(tb, 3), scalar2=None, op0=ALU.mult)
                    vector.tensor_scalar(out=S["vsc"][:, 11:12], in0=st(tb, 5),
                                         scalar1=st(tb, 3), scalar2=128.5,
                                         op0=ALU.mult, op1=ALU.add)
                    vector.drain().then_inc(dv.h, 1)
                EV(_am); dv.v += 1
                t_am = dv.v

                sbuf_sc = S["sd"][:, 0:1024]
                for var in range(3):
                    if var > 0:
                        pe_now = pe.v
                        def _sc(scalar, tb=tb, var=var, t_am=t_am, pe_now=pe_now):
                            W(scalar, (dv, t_am), (pe, pe_now))
                            if var == 1:
                                scalar.activation(out=S["sd"][:, 0:1024],
                                                  in_=S["xsb"][:, tb * D: (tb + 1) * D],
                                                  func=AF.Identity, scale=S["vsc"][:, 8:9],
                                                  bias=S["vsc"][:, 9:10])
                            else:
                                scalar.activation(out=S["sd"][:, 0:1024],
                                                  in_=S["xsb"][:, tb * D: (tb + 1) * D],
                                                  func=AF.Identity, scale=S["vsc"][:, 10:11],
                                                  bias=S["vsc"][:, 11:12])
                            scalar.drain().then_inc(ac.h, 1)
                        EA(_sc); ac.v += 1
                        t_src = ac.v
                        src = sbuf_sc
                    else:
                        t_src = t_xn[tb]
                        src = S["xn_t"][:]
                    for grp in range(2):
                        bank = "a" if ((var * 2 + grp) % 2 == 0) else "b"
                        psu = PS[bank]
                        def _tp(tensor, var=var, tb=tb, grp=grp, psu=psu, src=src,
                                t_src=t_src, bw=psfree[bank]):
                            W(tensor, (ac, t_src))
                            if bw is not None:
                                W(tensor, bw)
                            for j in range(4):
                                kc = grp * 4 + j
                                tensor.transpose(out=psu[:, j * 128: (j + 1) * 128],
                                                 in_=src[:, kc * 128: (kc + 1) * 128],
                                                 identity=S["ident"][:])
                            tensor.drain().then_inc(pe.h, 1)
                        EP(_tp); pe.v += 1
                        t_tp = pe.v

                        if var == 0:
                            def _ev(scalar, tb=tb, grp=grp, psu=psu, t_tp=t_tp):
                                W(scalar, (pe, t_tp))
                                for j in range(4):
                                    kc = grp * 4 + j
                                    scalar.copy(
                                        out=S["xnf"][:, kc * SEG + tb * 128: kc * SEG + tb * 128 + 128],
                                        in_=psu[:, j * 128: (j + 1) * 128])
                                scalar.drain().then_inc(ac.h, 1)
                            EA(_ev); ac.v += 1
                            psfree[bank] = (ac, ac.v)
                        elif var == 1:
                            def _ev(vector, tb=tb, grp=grp, psu=psu, t_tp=t_tp):
                                W(vector, (pe, t_tp))
                                for j in range(4):
                                    kc = grp * 4 + j
                                    vector.tensor_scalar(
                                        out=S["xn2f"][:, kc * SEG + tb * 128: kc * SEG + tb * 128 + 128],
                                        in0=psu[:, j * 128: (j + 1) * 128],
                                        scalar1=S["nwv"][:, kc: kc + 1], scalar2=None, op0=ALU.mult)
                                vector.drain().then_inc(dv.h, 1)
                            EV(_ev); dv.v += 1
                            psfree[bank] = (dv, dv.v)
                        else:
                            def _ev(scalar, tb=tb, grp=grp, psu=psu, t_tp=t_tp, ws=t_xqf_prev,
                                    wso=tk.get("ep_a_free", 0)):
                                W(scalar, (pe, t_tp))
                                if ws:
                                    scalar.wait_ge(dv.h, ws)
                                if wso:
                                    scalar.wait_ge(so.h, wso)
                                scalar.copy(
                                    out=S["ystrip"][:, tb * 1024 + grp * 512: tb * 1024 + grp * 512 + 512],
                                    in_=psu[:])
                                scalar.drain().then_inc(ac.h, 1)
                            EA(_ev); ac.v += 1
                            psfree[bank] = (ac, ac.v)
                tk["xn_t_free_pe"] = pe.v
                tk["xn_t_free_dv"] = t_am
            tk["xsb_free_ac"] = 0
            t_y = ac.v
            def _strip(vector, t_y=t_y, we=tk["epf_free_ac"], wg=tk.get("gwr", 0)):
                W(vector, (ac, t_y))
                if we:
                    vector.wait_ge(ac.h, we)
                if wg:
                    vector.wait_ge(gw.h, wg)
                tt = S["ystrip"][:, 0:2048]
                rr = S["sd"][:, 0:2048]
                dd = S["zone"][:, 4096:6144]
                vector.tensor_copy(out=rr, in_=tt)
                for k in range(7, -1, -1):
                    sv = float(1 << k)
                    vector.drain()
                    vector.tensor_scalar(out=dd, in0=rr, scalar1=sv, scalar2=-sv,
                                         op0=ALU.is_ge, op1=ALU.mult)
                    vector.drain()
                    vector.tensor_tensor(out=rr, in0=rr, in1=dd, op=ALU.add)
                vector.drain()
                vector.tensor_scalar(out=tt, in0=tt, scalar1=-128.0, scalar2=None, op0=ALU.add)
                vector.drain()
                for tb in range(2):
                    tt_src = S["ystrip"][:, tb * 1024: (tb + 1) * 1024].rearrange("p (kc t) -> p kc t", kc=8)
                    rr_src = S["sd"][:, tb * 1024: (tb + 1) * 1024].rearrange("p (kc t) -> p kc t", kc=8)
                    o_ap = S["xqf"][:].rearrange("p (kc t) -> p kc t", kc=8)[:, :, tb * 128: tb * 128 + 128]
                    vector.tensor_tensor(out=o_ap, in0=tt_src, in1=rr_src, op=ALU.subtract)
                vector.drain().then_inc(dv.h, 1)
            EV(_strip); dv.v += 1
            tk["strip_free"] = dv.v
            t_xqf = dv.v

            if ph < 2:
                continue
            t_wq_ready = t_wload
            elu_fr = {"1": 0, "2": 0}
            t_xn2 = dv.v
            t_xnf = ac.v

            for proj, wname, dst in (("q", "wq", "pqf"), ("k", "wk", "pkf")):
                for m in range(KC):
                    bank = "a" if (m % 2 == 0) else "b"
                    psu = PS[bank]
                    def _mm(tensor, wname=wname, m=m, psu=psu, bw=psfree[bank],
                            t_xn2=t_xn2, t_wq_ready=t_wq_ready):
                        W(tensor, (dv, t_xn2), (wt, t_wq_ready))
                        if bw is not None:
                            W(tensor, bw)
                        for k in range(KC):
                            tensor.matmul(out=psu[:, 0:SEG],
                                          lhsT=S[wname][:, k * D + m * 128: k * D + (m + 1) * 128],
                                          rhs=S["xn2f"][:, k * SEG: (k + 1) * SEG],
                                          start=(k == 0), stop=(k == KC - 1))
                        tensor.drain().then_inc(pe.h, 1)
                    EP(_mm); pe.v += 1
                    t_mm = pe.v

                    def _e1(vector, psu=psu, t_mm=t_mm, w1=elu_fr["1"]):
                        W(vector, (pe, t_mm))
                        if w1:
                            vector.wait_ge(ac.h, w1)
                        vector.tensor_scalar(out=S["elu1"][:, 0:SEG], in0=psu[:, 0:SEG],
                                             scalar1=0.0, scalar2=None, op0=ALU.min)
                        vector.drain().then_inc(dv.h, 1)
                    EV(_e1); dv.v += 1
                    t_e1 = dv.v

                    def _e2(scalar, t_e1=t_e1, w2=elu_fr["2"]):
                        W(scalar, (dv, t_e1))
                        if w2:
                            scalar.wait_ge(dv.h, w2)
                        scalar.activation(out=S["elu2"][:, 0:SEG], in_=S["elu1"][:, 0:SEG], func=AF.Exp)
                        scalar.drain().then_inc(ac.h, 1)
                    EA(_e2); ac.v += 1
                    t_e2 = ac.v
                    elu_fr["1"] = t_e2

                    def _e3(vector, psu=psu, m=m, dst=dst, t_e2=t_e2):
                        W(vector, (ac, t_e2))
                        vector.scalar_tensor_tensor(out=S[dst][:, m * SEG: (m + 1) * SEG],
                                                    in0=psu[:, 0:SEG], scalar=0.0,
                                                    in1=S["elu2"][:, 0:SEG],
                                                    op0=ALU.max, op1=ALU.add)
                        vector.drain().then_inc(dv.h, 1)
                    EV(_e3); dv.v += 1
                    elu_fr["2"] = dv.v
                    psfree[bank] = (dv, dv.v)
            t_pq = dv.v
            t_pk = dv.v

            for m in range(KC):
                bank = "a" if (m % 2 == 0) else "b"
                psu = PS[bank]
                def _mmu(tensor, m=m, psu=psu, bw=psfree[bank], t_xnf=t_xnf, t_wq_ready=t_wq_ready):
                    W(tensor, (ac, t_xnf), (wt, t_wq_ready))
                    if bw is not None:
                        W(tensor, bw)
                    for k in range(KC):
                        tensor.matmul(out=psu[:, 0:SEG],
                                      lhsT=S["wu"][:, k * D + m * 128: k * D + (m + 1) * 128],
                                      rhs=S["xnf"][:, k * SEG: (k + 1) * SEG],
                                      start=(k == 0), stop=(k == KC - 1))
                    tensor.drain().then_inc(pe.h, 1)
                EP(_mmu); pe.v += 1
                t_mm = pe.v

                def _evu(scalar, m=m, psu=psu, t_mm=t_mm):
                    W(scalar, (pe, t_mm))
                    scalar.copy(out=S["uf"][:, m * SEG: (m + 1) * SEG], in_=psu[:, 0:SEG])
                    scalar.drain().then_inc(ac.h, 1)
                EA(_evu); ac.v += 1
                psfree[bank] = (ac, ac.v)
            t_uf = ac.v

            for tb in range(2):
                for half in range(2):
                    bank = "a" if ((tb * 2 + half) % 2 == 0) else "b"
                    psu = PS[bank]
                    def _mmv(tensor, tb=tb, half=half, psu=psu, bw=psfree[bank],
                             t_xn2=t_xn2, t_wq_ready=t_wq_ready):
                        W(tensor, (dv, t_xn2), (wt, t_wq_ready))
                        if bw is not None:
                            W(tensor, bw)
                        for k in range(KC):
                            tensor.matmul(out=psu[:],
                                          lhsT=S["xn2f"][:, k * SEG + tb * 128: k * SEG + (tb + 1) * 128],
                                          rhs=S["wv"][:, k * D + half * 512: k * D + half * 512 + 512],
                                          start=(k == 0), stop=(k == KC - 1))
                        tensor.drain().then_inc(pe.h, 1)
                    EP(_mmv); pe.v += 1
                    t_mm = pe.v

                    def _evv(vector, tb=tb, half=half, psu=psu, t_mm=t_mm):
                        W(vector, (pe, t_mm))
                        base = tb * 1040 + half * 520
                        vector.tensor_copy(
                            out=S["vtm"][:, base: base + 520].rearrange(
                                "p (h f) -> p h f", f=65)[:, :, 0:64],
                            in_=psu[:].rearrange("p (h f) -> p h f", f=64))
                        vector.drain().then_inc(dv.h, 1)
                    EV(_evv); dv.v += 1
                    psfree[bank] = (dv, dv.v)
            t_vtm = dv.v

            for step in range(4):
                tb, grp = divmod(step, 2)
                bank = "a" if (step % 2 == 0) else "b"
                psu = PS[bank]
                def _tpk(tensor, tb=tb, grp=grp, psu=psu, bw=psfree[bank], t_pk=t_pk):
                    W(tensor, (dv, t_pk))
                    if bw is not None:
                        W(tensor, bw)
                    pb = psu[:].bitcast(BF16)
                    for j in range(4):
                        rho = grp * 4 + j
                        tensor.transpose(out=pb[:, j * 128: (j + 1) * 128],
                                         in_=S["pkf"][:, rho * SEG + tb * 128: rho * SEG + (tb + 1) * 128],
                                         identity=S["identb"][:])
                    tensor.drain().then_inc(pe.h, 1)
                EP(_tpk); pe.v += 1
                t_mm = pe.v

                def _etk(scalar, tb=tb, grp=grp, psu=psu, t_mm=t_mm):
                    W(scalar, (pe, t_mm))
                    pb = psu[:].bitcast(BF16)
                    for j in range(4):
                        rho = grp * 4 + j
                        scalar.copy(out=S["pktm"][:, tb * 1024 + rho * 128: tb * 1024 + (rho + 1) * 128],
                                    in_=pb[:, j * 128: (j + 1) * 128])
                    scalar.drain().then_inc(ac.h, 1)
                EA(_etk); ac.v += 1
                psfree[bank] = (ac, ac.v)
            t_pktm = ac.v

            def _mst(vector, c=c, t_xnf=t_xnf, wpe=pe_prev):
                W(vector, (ac, t_xnf))
                if wpe:
                    vector.wait_ge(pe.h, wpe)
                with nc.allow_low_precision(reason="m_stack sums stored bf16 as matmul operand"):
                    for kc in range(KC):
                        vector.tensor_reduce(out=S["mst"][:, kc * 8 + c: kc * 8 + c + 1],
                                             in_=S["xnf"][:, kc * SEG: (kc + 1) * SEG],
                                             axis=AX.X, op=ALU.add)
                vector.drain().then_inc(dv.h, 1)
            EV(_mst); dv.v += 1
            t_mst = dv.v

            if ph < 3:
                continue
            def _s2t(tensor, t_xnf=t_xnf, t_uf=t_uf, bw=psfree["a"]):
                W(tensor, (ac, max(t_xnf, t_uf)))
                if bw is not None:
                    W(tensor, bw)
                for k in range(KC):
                    tensor.matmul(out=ps_a[:, 0:256],
                                  lhsT=S["xnf"][:, k * SEG: k * SEG + 128],
                                  rhs=S["uf"][:, k * SEG: (k + 1) * SEG],
                                  start=(k == 0), stop=(k == KC - 1))
                for k in range(KC):
                    tensor.matmul(out=ps_a[:, 256:384],
                                  lhsT=S["xnf"][:, k * SEG + 128: (k + 1) * SEG],
                                  rhs=S["uf"][:, k * SEG + 128: (k + 1) * SEG],
                                  start=(k == 0), stop=(k == KC - 1))
                tensor.drain().then_inc(pe.h, 1)
            EP(_s2t); pe.v += 1
            t_s2t = pe.v

            def _s2e(scalar, t_s2t=t_s2t):
                W(scalar, (pe, t_s2t))
                scalar.copy(out=S["s2m"][:, 0:384], in_=ps_a[:, 0:384])
                scalar.drain().then_inc(ac.h, 1)
            EA(_s2e); ac.v += 1
            t_s2e = ac.v

            def _s2mk(vector, t_s2e=t_s2e):
                W(vector, (ac, t_s2e))
                vector.tensor_tensor(out=S["s2m"][:, 0:128], in0=S["s2m"][:, 0:128],
                                     in1=S["maskA"][:], op=ALU.mult)
                vector.tensor_tensor(out=S["s2m"][:, 256:384], in0=S["s2m"][:, 256:384],
                                     in1=S["maskA"][:], op=ALU.mult)
                vector.drain().then_inc(dv.h, 1)
            EV(_s2mk); dv.v += 1
            t_s2m = dv.v
            psfree["a"] = (ac, t_s2e)

            def _scmm(tensor, c=c, t_s2m=t_s2m, t_mst=t_mst, t_uf=t_uf, bw=psfree["b"]):
                W(tensor, (dv, max(t_s2m, t_mst)), (ac, t_uf))
                if bw is not None:
                    W(tensor, bw)
                ones = S["vtm"][:, 64:65]
                tensor.matmul(out=ps_b[:, 8:9], lhsT=S["s2m"][:, 0:128], rhs=ones,
                              start=True, stop=True)
                tensor.matmul(out=ps_b[:, 24:25], lhsT=S["s2m"][:, 128:256], rhs=ones,
                              start=True, stop=False, skip_group_check=True)
                tensor.matmul(out=ps_b[:, 24:25], lhsT=S["s2m"][:, 256:384], rhs=ones,
                              start=False, stop=True, skip_group_check=True)
                if c > 0:
                    for tb in range(2):
                        for k in range(KC):
                            tensor.matmul(out=ps_b[:, tb * 16: tb * 16 + c],
                                          lhsT=S["uf"][:, k * SEG + tb * 128: k * SEG + (tb + 1) * 128],
                                          rhs=S["mst"][:, k * 8: k * 8 + c],
                                          start=(k == 0), stop=(k == KC - 1),
                                          skip_group_check=True)
                tensor.drain().then_inc(pe.h, 1)
            EP(_scmm); pe.v += 1
            t_scmm = pe.v

            def _scev(vector, c=c, t_scmm=t_scmm):
                W(vector, (pe, t_scmm))
                for tb in range(2):
                    if c > 0:
                        vector.tensor_scalar(out=S["sc"][:, tb * 16: tb * 16 + c],
                                             in0=ps_b[:, tb * 16: tb * 16 + c],
                                             scalar1=1.0 / 8192.0, scalar2=None, op0=ALU.mult)
                    if c < 8:
                        vector.memset(S["sc"][:, tb * 16 + c: tb * 16 + 8], NEG)
                    vector.tensor_scalar(out=S["sc"][:, tb * 16 + 8: tb * 16 + 9],
                                         in0=ps_b[:, tb * 16 + 8: tb * 16 + 9],
                                         scalar1=S["vsc"][:, tb: tb + 1], scalar2=None, op0=ALU.mult)
                vector.drain()
                for tb in range(2):
                    vector.tensor_reduce(out=S["vsc"][:, 26 + tb: 27 + tb],
                                         in_=S["sc"][:, tb * 16: tb * 16 + 9],
                                         axis=AX.X, op=ALU.max, negate=True)
                vector.drain().then_inc(dv.h, 1)
            EV(_scev); dv.v += 1
            t_scev = dv.v
            psfree["b"] = (dv, t_scev)

            def _scexp(scalar, t_scev=t_scev):
                W(scalar, (dv, t_scev))
                for tb in range(2):
                    scalar.activation(out=S["aw"][:, tb * 16: tb * 16 + 9],
                                      in_=S["sc"][:, tb * 16: tb * 16 + 9], func=AF.Exp,
                                      bias=S["vsc"][:, 26 + tb: 27 + tb],
                                      accum_out=S["vsc"][:, 28 + tb: 29 + tb])
                scalar.drain().then_inc(ac.h, 1)
            EA(_scexp); ac.v += 1
            t_sce = ac.v

            def _scn(vector, t_sce=t_sce):
                W(vector, (ac, t_sce))
                for tb in range(2):
                    vector.reciprocal(out=S["vsc"][:, 30 + tb: 31 + tb],
                                      in_=S["vsc"][:, 28 + tb: 29 + tb])
                vector.drain()
                for tb in range(2):
                    vector.tensor_scalar(out=S["aw"][:, tb * 16: tb * 16 + 9],
                                         in0=S["aw"][:, tb * 16: tb * 16 + 9],
                                         scalar1=S["vsc"][:, 30 + tb: 31 + tb],
                                         scalar2=None, op0=ALU.mult)
                vector.drain()
                for tb in range(2):
                    vector.tensor_copy(out=st(tb, 6), in_=S["aw"][:, tb * 16 + 8: tb * 16 + 9])
                vector.drain().then_inc(dv.h, 1)
            EV(_scn); dv.v += 1
            t_aw = dv.v

            t_gt_prev = tk.get("awp_free", 0)
            for tb in range(2):
                def _gtp0(vector, tb=tb, t_aw=t_aw, wpe=t_gt_prev):
                    W(vector, (dv, t_aw))
                    if wpe:
                        vector.wait_ge(pe.h, wpe)
                    vector.tensor_copy(out=S["awp"][:, 0:9], in_=S["aw"][:, tb * 16: tb * 16 + 9])
                    vector.drain().then_inc(dv.h, 1)
                EV(_gtp0); dv.v += 1
                t_awp = dv.v

                def _gtp(tensor, t_awp=t_awp, bw=psfree["a"]):
                    W(tensor, (dv, t_awp))
                    if bw is not None:
                        W(tensor, bw)
                    tensor.transpose(out=ps_a[0:16, 0:128], in_=S["awp"][:, 0:16],
                                     identity=S["ident"][:])
                    tensor.drain().then_inc(pe.h, 1)
                EP(_gtp); pe.v += 1
                t_gt = pe.v
                t_gt_prev = t_gt
                tk["awp_free"] = t_gt

                def _gte(vector, tb=tb, t_gt=t_gt, wg=tk.get("gwr", 0)):
                    W(vector, (pe, t_gt))
                    if wg:
                        vector.wait_ge(gw.h, wg)
                    vector.tensor_copy(out=S["gcm"][0:16, tb * 128: (tb + 1) * 128],
                                       in_=ps_a[0:16, 0:128])
                    vector.drain().then_inc(dv.h, 1)
                EV(_gte); dv.v += 1
                psfree["a"] = (dv, dv.v)
            t_gcm = dv.v

            def _gwr(gpsimd, tok0=tok0, t_gcm=t_gcm):
                W(gpsimd, (dv, t_gcm))
                gpsimd.dma_start(out=scr_g[:, tok0: tok0 + SEG],
                                 in_=S["gcm"][0:8, 0:SEG]).then_inc(gw.h, 16)
            EG(_gwr); gw.v += 16
            t_scrg = gw.v
            tk["gwr"] = t_scrg

            if ph < 4:
                continue
            for bl in range(2):
                slot = si * 2 + bl

                def _smm(tensor, bl=bl, slot=slot, t_pktm=t_pktm, t_vtm=t_vtm,
                         sw=psfree["scum"]):
                    W(tensor, (ac, t_pktm), (dv, t_vtm))
                    if sw is not None:
                        W(tensor, sw)
                    for h in range(H):
                        tensor.matmul(
                            out=ps_scum[0:64, 128 * h: 128 * h + 65],
                            lhsT=S["pktm"][:, bl * 1024 + (h // 2) * 128 + (h % 2) * 64:
                                           bl * 1024 + (h // 2) * 128 + (h % 2) * 64 + 64],
                            rhs=S["vtm"][:, bl * 1040 + 65 * h: bl * 1040 + 65 * h + 65],
                            start=(slot == 0 and h % 4 == 0), stop=(slot == 15),
                            skip_group_check=True)
                    tensor.drain().then_inc(pe.h, 1)
                EP(_smm); pe.v += 1
                t_smm = pe.v

                def _scp(vector, slot=slot, t_smm=t_smm):
                    W(vector, (pe, t_smm))
                    vector.tensor_copy(
                        out=S["stack"][0:64, slot * 1040: (slot + 1) * 1040],
                        in_=ps_scum[0:64, :].rearrange("p (h f) -> p h f", h=16)[:, :, 0:65])
                    vector.drain().then_inc(dv.h, 1)
                EV(_scp); dv.v += 1
                tk["cp"][slot] = dv.v
                psfree["scum"] = (dv, dv.v)

                dch = da if slot % 2 == 0 else db
                def _sdp(gpsimd, slot=slot, w=tk["cp"][slot], dch=dch):
                    W(gpsimd, (dv, w))
                    gpsimd.dma_start(out=S["stack"][64:128, slot * 1040: (slot + 1) * 1040],
                                     in_=S["stack"][0:64, slot * 1040: (slot + 1) * 1040]
                                     ).then_inc(dch.h, 16)
                EG(_sdp); dch.v += 16
                tk["dup"][slot] = (dch, dch.v)

                t_atm_prev = 0
                for g in range(4 if ph >= 41 else 0):
                    if ph == 410 and g > 0:
                        break
                    def _atmm(tensor, bl=bl, g=g, t_pq=t_pq, t_pk=t_pk, bw=psfree["a"]):
                        W(tensor, (dv, max(t_pq, t_pk)))
                        if bw is not None:
                            W(tensor, bw)
                        for j in range(4):
                            h = 4 * g + j
                            rho, par = h // 2, (h % 2) * 64
                            tensor.wait_ge(gp.h, 1)
                            tensor.matmul(
                                out=ps_a[:, j * 128: (j + 1) * 128],
                                lhsT=S["pkf"][par: par + 64, rho * SEG + bl * 128: rho * SEG + bl * 128 + 128],
                                rhs=S["pqf"][par: par + 64, rho * SEG + bl * 128: rho * SEG + bl * 128 + 128],
                                start=True, stop=True)
                        tensor.drain().then_inc(pe.h, 1)
                    EP(_atmm); pe.v += 1
                    t_at = pe.v

                    if ph == 410:
                        continue
                    def _atc(scalar, t_at=t_at, wm=t_atm_prev, t_scrg=t_scrg):
                        W(scalar, (pe, t_at), (gw, t_scrg))
                        if wm:
                            scalar.wait_ge(dv.h, wm)
                        scalar.copy(out=S["atb"][:], in_=ps_a[:])
                        scalar.drain().then_inc(ac.h, 1)
                    EA(_atc); ac.v += 1
                    t_atc = ac.v
                    psfree["a"] = (ac, t_atc)

                    def _atk(vector, t_atc=t_atc):
                        W(vector, (ac, t_atc))
                        vector.tensor_tensor(
                            out=S["atm"][:],
                            in0=S["atb"][:],
                            in1=S["maskA"][:].unsqueeze(1).broadcast_to([128, 4, 128]),
                            op=ALU.mult)
                        vector.drain().then_inc(dv.h, 1)
                    EV(_atk); dv.v += 1
                    t_atm = dv.v
                    t_atm_prev = t_atm

                    if (ph < 42 or ph == 411) and ph != 420:
                        continue
                    def _ii(tensor, bl=bl, g=g, slot=slot, t_atm=t_atm, t_vtm=t_vtm,
                            wdup=(tk["dup"].get(slot - 1, None)),
                            bwo=psfree["o"], bwb=psfree["b"]):
                        W(tensor, (dv, t_atm))
                        if slot > 0 and wdup is not None:
                            W(tensor, wdup)
                        if g == 0:
                            if bwo is not None:
                                W(tensor, bwo)
                            if bwb is not None:
                                W(tensor, bwb)
                        for j in range(4):
                            h = 4 * g + j
                            rho, par = h // 2, (h % 2) * 64
                            bk, off = out_off(h)
                            pout = PS[bk]
                            first = True
                            if slot > 0:
                                tensor.wait_ge(gp.h, 1)
                                tensor.matmul(
                                    out=pout[:, off: off + 65],
                                    lhsT=S["pqf"][par: par + 64,
                                                  rho * SEG + bl * 128: rho * SEG + bl * 128 + 128],
                                    rhs=S["stack"][par: par + 64,
                                                   (slot - 1) * 1040 + 65 * h: (slot - 1) * 1040 + 65 * h + 65],
                                    start=True, stop=False, skip_group_check=True)
                                first = False
                            tensor.wait_ge(gp.h, 1)
                            tensor.matmul(
                                out=pout[:, off: off + 65],
                                lhsT=S["atm"][:, j * 128: (j + 1) * 128],
                                rhs=S["vtm"][:, bl * 1040 + 65 * h: bl * 1040 + 65 * h + 65],
                                start=first, stop=True, skip_group_check=True)
                        tensor.drain().then_inc(pe.h, 1)
                    EP(_ii); pe.v += 1
                t_ii = pe.v

                if ph < 42 or ph in (411, 420):
                    continue
                def _dn(vector, bl=bl, t_ii=t_ii):
                    W(vector, (pe, t_ii))
                    vector.tensor_copy(
                        out=S["dvec"][:, 0:7],
                        in_=ps_o[:, 0:455].rearrange("p (h f) -> p h f", f=65)[:, :, 64:65])
                    vector.tensor_copy(
                        out=S["dvec"][:, 7:14],
                        in_=ps_o[:, 512:967].rearrange("p (h f) -> p h f", f=65)[:, :, 64:65])
                    vector.tensor_copy(
                        out=S["dvec"][:, 14:16],
                        in_=ps_b[:, 0:130].rearrange("p (h f) -> p h f", f=65)[:, :, 64:65])
                    vector.drain()
                    vector.tensor_scalar(out=S["dvec"][:], in0=S["dvec"][:], scalar1=EPS_ATT,
                                         scalar2=None, op0=ALU.add)
                    vector.drain()
                    vector.reciprocal(out=S["rvec"][:], in_=S["dvec"][:])
                    vector.drain()
                    for psu, base, h0, cnt in ((ps_o, 0, 0, 7), (ps_o, 512, 7, 7), (ps_b, 0, 14, 2)):
                        vector.tensor_tensor(
                            out=S["ar"][:].rearrange("p (b h f) -> p b h f", b=2, h=16)[
                                :, bl, h0: h0 + cnt, :],
                            in0=psu[:, base: base + cnt * 65].rearrange(
                                "p (h f) -> p h f", f=65)[:, :, 0:64],
                            in1=S["rvec"][:, h0: h0 + cnt].unsqueeze(2).broadcast_to(
                                [128, cnt, 64]),
                            op=ALU.mult)
                    vector.drain().then_inc(dv.h, 1)
                EV(_dn); dv.v += 1
                psfree["o"] = (dv, dv.v)
                psfree["b"] = (dv, dv.v)
            t_ar = dv.v

            if ph < 5 or ph in (40, 41, 42, 410, 411, 420):
                continue
            for step in range(4):
                bl, grp = divmod(step, 2)
                bank = "a" if (step % 2 == 0) else "b"
                psu = PS[bank]
                def _tar(tensor, bl=bl, grp=grp, psu=psu, bw=psfree[bank], t_ar=t_ar):
                    W(tensor, (dv, t_ar))
                    if bw is not None:
                        W(tensor, bw)
                    pb = psu[:].bitcast(BF16)
                    for j in range(4):
                        kc = grp * 4 + j
                        tensor.transpose(out=pb[:, j * 128: (j + 1) * 128],
                                         in_=S["ar"][:, bl * 1024 + kc * 128: bl * 1024 + (kc + 1) * 128],
                                         identity=S["identb"][:])
                    tensor.drain().then_inc(pe.h, 1)
                EP(_tar); pe.v += 1
                t_mm = pe.v

                def _ear(scalar, bl=bl, grp=grp, psu=psu, t_mm=t_mm):
                    W(scalar, (pe, t_mm))
                    pb = psu[:].bitcast(BF16)
                    for j in range(4):
                        kc = grp * 4 + j
                        scalar.copy(out=S["arf"][:, kc * SEG + bl * 128: kc * SEG + (bl + 1) * 128],
                                    in_=pb[:, j * 128: (j + 1) * 128])
                    scalar.drain().then_inc(ac.h, 1)
                EA(_ear); ac.v += 1
                psfree[bank] = (ac, ac.v)
            t_arf = ac.v
            t_ph4 = pe.v

            if ph < 6:
                continue
            t_xrl = 0

            t_pgt = {}
            t_hmmn = {}
            if c > 0:
                for n in range(c):
                    nb = n % 2
                    gch = ga if nb == 0 else gb
                    def _gld(gpsimd, n=n, nb=nb, tok0=tok0, t_scrg=t_scrg, gch=gch,
                             wfree=max(t_pgt.get(n - 2, tk.get("gstr_free", 0)), t_vtm)):
                        W(gpsimd, (gw, t_scrg))
                        if wfree:
                            gpsimd.wait_ge(dv.h, wfree)
                        gpsimd.dma_start(out=S["gstr"][:, nb * SEG: (nb + 1) * SEG],
                                         in_=scr_g[n: n + 1, tok0: tok0 + SEG].broadcast_to(
                                             [128, SEG])).then_inc(gch.h, 16)
                    EG(_gld); gch.v += 16
                    t_gld = gch.v

                    def _pgt(vector, n=n, nb=nb, t_gld=t_gld, t_pq=t_pq, gch=gch,
                             wpe=(t_ph4 if n < 2 else t_hmmn[n - 2])):
                        W(vector, (gch, t_gld), (dv, t_pq))
                        vector.wait_ge(pe.h, wpe)
                        for rho in range(KC):
                            vector.tensor_tensor(
                                out=pgt_buf(nb)[:, rho * SEG: (rho + 1) * SEG],
                                in0=S["pqf"][:, rho * SEG: (rho + 1) * SEG],
                                in1=S["gstr"][:, nb * SEG: (nb + 1) * SEG], op=ALU.mult)
                        vector.drain().then_inc(dv.h, 1)
                    EV(_pgt); dv.v += 1
                    t_pgt[n] = dv.v

                    def _hmm(tensor, n=n, nb=nb, c=c, t_p=t_pgt[n],
                             wdup=tk["dup"].get(2 * n + 1, None),
                             bwo=psfree["o"] if n == 0 else None,
                             bwa=psfree["a"] if n == 0 else None,
                             bwb=psfree["b"] if n == 0 else None):
                        W(tensor, (dv, t_p))
                        if wdup is not None:
                            W(tensor, wdup)
                        for bw in (bwo, bwa, bwb):
                            if bw is not None:
                                W(tensor, bw)
                        for h in range(H):
                            rho, par = h // 2, (h % 2) * 64
                            tensor.wait_ge(gp.h, 1)
                            for th in range(2):
                                if th == 0:
                                    o_ap = ps_o[:, 64 * h: 64 * h + 64]
                                else:
                                    o_ap = (ps_a[:, 64 * h: 64 * h + 64] if h < 8
                                            else ps_b[:, 64 * (h - 8): 64 * (h - 8) + 64])
                                tensor.matmul(
                                    out=o_ap,
                                    lhsT=pgt_buf(nb)[par: par + 64,
                                                     rho * SEG + th * 128:
                                                     rho * SEG + th * 128 + 128],
                                    rhs=S["stack"][par: par + 64,
                                                   (2 * n + 1) * 1040 + 65 * h:
                                                   (2 * n + 1) * 1040 + 65 * h + 64],
                                    start=(n == 0 and h % 8 == 0), stop=(n == c - 1),
                                    skip_group_check=True)
                        tensor.drain().then_inc(pe.h, 1)
                    EP(_hmm); pe.v += 1
                    t_hmmn[n] = pe.v
                t_hmm = pe.v
                tk["pgt_free"] = t_hmm
                tk["gstr_free"] = max(t_pgt.values()) if t_pgt else 0

                def _hbe(vector, t_hmm=t_hmm, t_scrg=t_scrg):
                    W(vector, (pe, t_hmm), (gw, t_scrg))
                    vector.tensor_copy(out=S["hb"][:, 0:512], in_=ps_a[:])
                    vector.tensor_copy(out=S["hb"][:, 512:1024], in_=ps_b[:])
                    vector.drain().then_inc(dv.h, 1)
                EV(_hbe); dv.v += 1
                psfree["a"] = (dv, dv.v)
                psfree["b"] = (dv, dv.v)
                t_hb = dv.v
            else:
                t_hb = 0

            for tb in range(2):
                for half in range(2):
                    bank = "a" if (half == 0) else "b"
                    psu = PS[bank]
                    def _apm(tensor, tb=tb, half=half, psu=psu, bw=psfree[bank],
                             t_arf=t_arf, t_wq_ready=t_wq_ready):
                        W(tensor, (ac, t_arf), (wt, t_wq_ready))
                        if bw is not None:
                            W(tensor, bw)
                        for k in range(KC):
                            tensor.matmul(out=psu[:],
                                          lhsT=S["arf"][:, k * SEG + tb * 128: k * SEG + (tb + 1) * 128],
                                          rhs=S["wo"][:, k * D + half * 512: k * D + half * 512 + 512],
                                          start=(k == 0), stop=(k == KC - 1))
                        tensor.drain().then_inc(pe.h, 1)
                    EP(_apm); pe.v += 1
                    t_mm = pe.v

                    def _ape(scalar, tb=tb, half=half, psu=psu, t_mm=t_mm,
                             wso=tk.get("ep_a_free", 0)):
                        W(scalar, (pe, t_mm))
                        if wso:
                            scalar.wait_ge(so.h, wso)
                        scalar.activation(out=S["ep_a"][:, half * 512: half * 512 + 512],
                                          in_=psu[:], func=AF.Identity, scale=st(tb, 6))
                        scalar.drain().then_inc(ac.h, 1)
                    EA(_ape); ac.v += 1
                    psfree[bank] = (ac, ac.v)
                t_apa = ac.v

                for half in range(2):
                    bank = "a" if (half == 0) else "b"
                    psu = PS[bank]
                    def _ffm(tensor, tb=tb, half=half, psu=psu, bw=psfree[bank],
                             t_xqf=t_xqf):
                        W(tensor, (dv, t_xqf))
                        if bw is not None:
                            W(tensor, bw)
                        for k in range(KC):
                            tensor.matmul(out=psu[:],
                                          lhsT=S["xqf"][:, k * SEG + tb * 128: k * SEG + (tb + 1) * 128],
                                          rhs=S["wf"][:, k * D + half * 512: k * D + half * 512 + 512],
                                          start=(k == 0), stop=(k == KC - 1))
                        tensor.drain().then_inc(pe.h, 1)
                    EP(_ffm); pe.v += 1
                    t_mm = pe.v

                    def _ffe(vector, tb=tb, half=half, psu=psu, t_mm=t_mm,
                             wsl=tk.get("ep_f_free", 0)):
                        W(vector, (pe, t_mm), (bc, t_wq2))
                        if wsl:
                            vector.wait_ge(ac.h, wsl)
                        vector.tensor_tensor(out=S["ep_f"][:, half * 512: half * 512 + 512],
                                             in0=psu[:], in1=S["wsb"][:, half * 512: half * 512 + 512],
                                             op=ALU.mult)
                        vector.drain().then_inc(dv.h, 1)
                    EV(_ffe); dv.v += 1
                    psfree[bank] = (dv, dv.v)
                t_f1 = dv.v

                def _f2(vector, tb=tb, t_f1=t_f1):
                    vector.drain()
                    vector.scalar_tensor_tensor(out=S["ep_f"][:], in0=S["ep_f"][:],
                                                scalar=st(tb, 4), in1=S["bbc"][:],
                                                op0=ALU.mult, op1=ALU.add)
                    vector.drain().then_inc(dv.h, 1)
                EV(_f2); dv.v += 1
                t_f2 = dv.v

                def _sil(scalar, t_f2=t_f2):
                    W(scalar, (dv, t_f2))
                    scalar.activation(out=S["ep_g"][:], in_=S["ep_f"][:], func=AF.Exp,
                                      scale=-1.0)
                    scalar.drain().then_inc(ac.h, 1)
                EA(_sil); ac.v += 1
                t_sile = ac.v
                tk["ep_f_free"] = t_sile
                tk["epf_free_ac"] = t_sile

                def _sil2(vector, t_sile=t_sile):
                    W(vector, (ac, t_sile))
                    vector.tensor_scalar(out=S["ep_g"][:], in0=S["ep_g"][:],
                                         scalar1=1.0, scalar2=None, op0=ALU.add)
                    vector.drain()
                    vector.reciprocal(out=S["ep_g"][:], in_=S["ep_g"][:])
                    vector.drain()
                    vector.tensor_tensor(out=S["ep_g"][:], in0=S["ep_g"][:],
                                         in1=S["ep_f"][:], op=ALU.mult)
                    vector.drain().then_inc(dv.h, 1)
                EV(_sil2); dv.v += 1
                t_sil = dv.v

                def _gg(vector, tb=tb, t_aw=t_aw, ws1=tk.get("s1_done", 0)):
                    if ws1:
                        vector.wait_ge(ac.h, ws1)
                    vector.tensor_scalar(out=S["vsc"][:, 24:25], in0=st(tb, 6),
                                         scalar1=st(tb, 1), scalar2=None, op0=ALU.mult)
                    vector.tensor_scalar(out=S["vsc"][:, 25:26], in0=st(tb, 6),
                                         scalar1=st(tb, 5), scalar2=None, op0=ALU.mult)
                    vector.drain().then_inc(dv.h, 1)
                EV(_gg); dv.v += 1
                t_gg = dv.v

                def _s1(scalar, tb=tb, t_gg=t_gg):
                    W(scalar, (dv, t_gg))
                    scalar.activation(out=S["ep_e"][:], in_=S["xsb"][:, tb * D: (tb + 1) * D],
                                      func=AF.Identity, scale=S["vsc"][:, 24:25],
                                      bias=S["vsc"][:, 25:26])
                    scalar.drain().then_inc(ac.h, 1)
                EA(_s1); ac.v += 1
                t_s1 = ac.v
                tk["s1_done"] = t_s1
                tk["xsb_free_ac"] = t_s1

                def _epi(vector, tb=tb, c=c, t_s1=t_s1, t_apa=t_apa, t_sil=t_sil, t_hb=t_hb):
                    W(vector, (ac, max(t_s1, t_apa)))
                    vector.tensor_tensor(out=S["ep_e"][:], in0=S["ep_e"][:], in1=S["ep_a"][:],
                                         op=ALU.add)
                    vector.drain()
                    if c > 0:
                        if tb == 0:
                            vector.tensor_tensor(out=S["ep_e"][:], in0=S["ep_e"][:],
                                                 in1=ps_o[:], op=ALU.add)
                        else:
                            vector.tensor_tensor(out=S["ep_e"][:], in0=S["ep_e"][:],
                                                 in1=S["hb"][:], op=ALU.add)
                        vector.drain()
                    vector.tensor_tensor(out=S["ep_f"][:], in0=S["ep_e"][:], in1=S["ep_g"][:],
                                         op=ALU.mult)
                    vector.drain()
                    vector.tensor_tensor(out=S["ep_a"][:], in0=S["xsb"][:, tb * D: (tb + 1) * D],
                                         in1=S["ep_f"][:], op=ALU.add)
                    vector.drain().then_inc(dv.h, 1)
                EV(_epi); dv.v += 1
                t_epi = dv.v
                tk["xsb_free_dv"] = t_epi
                if tb == 0 and c > 0:
                    psfree["o"] = (dv, t_epi)

                def _out(gpsimd, b=b, tok0=tok0, tb=tb, t_epi=t_epi):
                    W(gpsimd, (dv, t_epi))
                    gpsimd.dma_start(out=out_d[b, tok0 + tb * 128: tok0 + (tb + 1) * 128, :],
                                     in_=S["ep_a"][:]).then_inc(so.h, 16)
                EG(_out); so.v += 16
                tk["ep_a_free"] = so.v

    if dbg_d:
        t_all = (dv.v, ac.v, pe.v)
        def _dbg(gpsimd, t_all=t_all):
            gpsimd.wait_ge(dv.h, t_all[0])
            gpsimd.wait_ge(ac.h, t_all[1])
            gpsimd.wait_ge(pe.h, t_all[2])
            for nm, src in dbg_done:
                gpsimd.dma_start(out=dbg_d[nm], in_=src).then_inc(so.h, 16)
        EG(_dbg)
        so.v += 16 * len(dbg_done)

    if so.v == 0:
        def _dummy(gpsimd):
            gpsimd.wait_ge(bc.h, 32)
            gpsimd.dma_start(out=out_d[0, 0:128, :], in_=S["bbc"][:]).then_inc(so.h, 16)
        EG(_dummy); so.v += 16
    fin = so.v
    def _fin(gpsimd, fin=fin):
        gpsimd.wait_ge(so.h, fin)
    EG(_fin)

    for eng in ("sync", "gpsimd", "vector", "scalar", "tensor"):
        fns = [f for e, f in _PEND if e == eng]
        def _emit_all(engine, fns=fns):
            for f in fns:
                f(engine)
        getattr(blk, eng)(_emit_all)

    blk.__exit__(None, None, None)
    return nc


_NC_CACHE = {}


def kernel(L0_attn_x, L0_attn_norm_w, L0_attn_Wq, L0_attn_Wk, L0_attn_Wv,
           L0_attn_Wo, L0_attn_Wu, L0_ffn_weight, L0_ffn_bias):
    x = np.ascontiguousarray(np.asarray(L0_attn_x, dtype=np.float32))
    wq = np.ascontiguousarray(np.asarray(L0_attn_Wq, dtype=np.float32).T)
    wk = np.ascontiguousarray(np.asarray(L0_attn_Wk, dtype=np.float32).T)
    wv = np.ascontiguousarray(np.asarray(L0_attn_Wv, dtype=np.float32).T)
    wo = np.ascontiguousarray(np.asarray(L0_attn_Wo, dtype=np.float32).T)
    wu = np.ascontiguousarray(np.asarray(L0_attn_Wu, dtype=np.float32).T)
    wf = np.ascontiguousarray(np.asarray(L0_ffn_weight, dtype=np.float32).T)
    wfn = np.ascontiguousarray(np.asarray(L0_ffn_weight, dtype=np.float32))
    nw = np.ascontiguousarray(np.asarray(L0_attn_norm_w, dtype=np.float32))
    fb = np.ascontiguousarray(np.asarray(L0_ffn_bias, dtype=np.float32))

    if "nc" not in _NC_CACHE:
        _NC_CACHE["nc"] = build()
    nc = _NC_CACHE["nc"]

    in_maps = []
    for core in range(8):
        in_maps.append({
            "x": np.ascontiguousarray(x[2 * core: 2 * core + 2]),
            "wqt": wq, "wkt": wk, "wvt": wv, "wot": wo, "wut": wu,
            "wft": wf, "wfn": wfn, "nw": nw, "fb": fb,
        })
    res = run_bass_kernel_spmd(nc, in_maps, core_ids=list(range(8)))
    out = np.concatenate([np.asarray(r["out"]) for r in res.results], axis=0)
    return out.astype(np.float32)


# version 08 (final)
# speedup vs baseline: 1.4039x; 1.0807x over previous
import numpy as np
import concourse.bass as bass
import concourse.mybir as mybir
from concourse.bass_utils import run_bass_kernel_spmd

F32 = mybir.dt.float32
BF16 = mybir.dt.bfloat16
AF = mybir.ActivationFunctionType
ALU = mybir.AluOpType
AX = mybir.AxisListType

BC, T, D, H, DK, SEG = 2, 2048, 1024, 16, 64, 256
NSB, KC = T // SEG, D // 128
EPS_ATT, EPS_LN, EPS_FFN = 1e-6, 1e-5, 1e-8
NEG = -30000.0


def out_off(h):
    if h < 7:
        return ("o", 65 * h)
    if h < 14:
        return ("o", 512 + 65 * (h - 7))
    return ("b", 65 * (h - 14))


def build(dbg_specs=(), n_units=16, ph=99):
    nc = bass.Bass("TRN2", target_bir_lowering=False)

    x_d = nc.dram_tensor("x", [BC, T, D], F32, kind="ExternalInput").ap()
    w_d = {nm: nc.dram_tensor(nm, [D, D], F32, kind="ExternalInput").ap()
           for nm in ("wqt", "wkt", "wvt", "wot", "wut", "wft", "wfn")}
    nw_d = nc.dram_tensor("nw", [D], F32, kind="ExternalInput").ap()
    fb_d = nc.dram_tensor("fb", [D], F32, kind="ExternalInput").ap()
    out_d = nc.dram_tensor("out", [BC, T, D], F32, kind="ExternalOutput").ap()
    scr_g = nc.dram_tensor("scr_g", [8, T], F32, kind="Internal").ap()
    scr_ws = nc.dram_tensor("scr_ws", [2, D], F32, kind="Internal").ap()
    dbg_d = {nm: nc.dram_tensor("dbg_" + nm, list(shape), dt, kind="ExternalOutput").ap()
             for nm, shape, dt in dbg_specs}
    dbg_done = []

    S = {}
    def al(name, shape, dtype=F32):
        S[name] = nc.alloc_sbuf_tensor(name, list(shape), dtype)

    for w in ("wq", "wk", "wv", "wu", "wo", "wf"):
        al(w, [128, KC * D], BF16)
    al("stack", [128, 16 * H * 65], BF16)
    for t in ("xnf", "xn2f", "xqf", "pqf", "pkf", "uf", "pktm"):
        al(t, [128, KC * SEG], BF16)
    al("vtm", [128, 2 * H * 65], BF16)
    al("xsb", [128, 2 * D], F32)
    al("xn_t", [128, D], F32)
    al("zone", [128, 6144], F32)
    Z = S["zone"]
    S["ystrip"] = Z[:, 0:2048]
    S["sd"] = Z[:, 2048:4096]
    S["d0"] = Z[:, 4096:5120]
    S["d1"] = Z[:, 5120:6144]
    S["ep_a"] = Z[:, 0:1024]
    S["ep_e"] = Z[:, 1024:2048]
    S["ep_f"] = Z[:, 2048:3072]
    S["ep_g"] = Z[:, 3072:4096]
    S["hb"] = Z[:, 4096:5120]
    S["elu1"] = Z[:, 5120:5632]
    S["elu2"] = Z[:, 5632:6144]
    S["rwsb"] = Z[:, 5120:6144]
    S["arf"] = S["xn2f"]
    S["ar"] = S["uf"][:]
    S["atb"] = Z[:, 4096:4352].bitcast(BF16)
    S["atm"] = Z[:, 4352:4608].bitcast(BF16)
    S["s2m"] = Z[:, 4608:4800].bitcast(BF16)
    S["gcm"] = bass.AP(Z[:, 4800:5056].tensor, Z[:, 4800:5056].offset,
                       [[1, 16], [1, 256]]) if False else Z[0:16, 4800:5056]
    S["gstr"] = Z[:, 5888:6144].bitcast(BF16)
    S["iotc"] = Z[:, 1024:1152]
    al("iotp", [128, 1], F32)
    al("ident", [128, 128], F32)
    al("identb", [128, 128], BF16)
    al("diag", [128, 128], F32)
    al("maskA", [128, 128], BF16)
    al("stats", [128, 16], F32)
    al("mst", [128, KC * 8], BF16)
    al("aw", [128, 32], F32)
    al("awp", [128, 16], F32)
    al("nwv", [128, KC], F32)
    al("bbc", [128, D], F32)
    al("wsb", [128, D], F32)
    al("dvec", [128, 16], F32)
    al("rvec", [128, 16], F32)
    al("vsc", [128, 32], F32)
    al("sc", [128, 32], F32)
    al("wsv", [128, KC], F32)
    al("cb", [128, 4], F32)

    def pgt_buf(nb):
        return S["uf"] if nb == 0 else S["xnf"]

    ps_scum = nc.alloc_psum_tensor("ps_scum", [64, 2048], F32)
    ps_a = nc.alloc_psum_tensor("ps_a", [128, 512], F32)
    ps_b = nc.alloc_psum_tensor("ps_b", [128, 512], F32)
    ps_o = nc.alloc_psum_tensor("ps_o", [128, 1024], F32)
    PS = {"a": ps_a, "b": ps_b, "o": ps_o}

    class Ctr:
        def __init__(self, name):
            self.h = nc.semaphore(name).__enter__()
            self.v = 0
    (ld, gd, pe, dv, ac, gp, so, xl, wa, wb, bc, wt, gw, da, db, ga, gb) = (
        Ctr(n) for n in ("ld", "gd", "pe", "dv", "ac", "gp", "so", "xl", "wa",
                         "wb", "bc", "wt", "gw", "da", "db", "ga", "gb"))

    blk = nc.Block().__enter__()
    _PEND = []
    def EV(fn): _PEND.append(("vector", fn))
    def EA(fn): _PEND.append(("scalar", fn))
    def EP(fn): _PEND.append(("tensor", fn))
    def EG(fn): _PEND.append(("gpsimd", fn))
    def ES(fn): _PEND.append(("sync", fn))

    def W(e, *pairs):
        for ctr, val in pairs:
            if val > 0:
                e.wait_ge(ctr.h, val)

    psfree = {"a": None, "b": None, "o": None, "scum": None}
    def bank_wait(e, *banks):
        for bk in banks:
            if psfree[bk] is not None:
                W(e, psfree[bk])

    def _s0(sync):
        sync.dma_start(out=S["nwv"][:], in_=nw_d.rearrange("(c p) -> p c", p=128),
                       allow_slow_non_contiguous=True).then_inc(bc.h, 16)
        sync.dma_start(out=S["bbc"][:], in_=fb_d.unsqueeze(0).broadcast_to([128, D])).then_inc(bc.h, 16)
    ES(_s0); bc.v += 32

    def _g0(gpsimd):
        for w, dn in (("wq", "wqt"), ("wk", "wkt"), ("wv", "wvt"), ("wu", "wut"), ("wo", "wot")):
            gpsimd.dma_start(out=S[w][:].rearrange("p (c d) -> p c d", d=D),
                             in_=w_d[dn].rearrange("(c p) d -> p c d", p=128)).then_inc(wt.h, 16)
        gpsimd.iota(S["iotc"][:], pattern=[[1, 128]], base=0, channel_multiplier=0,
                    allow_small_or_imprecise_dtypes=True)
        gpsimd.iota(S["iotp"][:], pattern=[[0, 1]], base=0, channel_multiplier=1,
                    allow_small_or_imprecise_dtypes=True)
        gpsimd.fence_inc(gp.h, 1)
    EG(_g0); wt.v += 80; gp.v += 1
    t_wload, t_iota = wt.v, gp.v

    def _v0(vector):
        vector.wait_ge(gp.h, t_iota)
        vector.tensor_scalar(out=S["ident"][:], in0=S["iotc"][:], scalar1=S["iotp"][:],
                             scalar2=None, op0=ALU.subtract)
        vector.drain()
        vector.tensor_scalar(out=S["maskA"][:], in0=S["ident"][:], scalar1=0.0,
                             scalar2=None, op0=ALU.is_ge)
        vector.tensor_scalar(out=S["ident"][:], in0=S["ident"][:], scalar1=0.0,
                             scalar2=None, op0=ALU.is_equal)
        vector.drain()
        vector.tensor_copy(out=S["identb"][:], in_=S["ident"][:])
        vector.tensor_scalar(out=S["vsc"][:, 2:3], in0=S["iotp"][:], scalar1=1.0,
                             scalar2=32.0, op0=ALU.add, op1=ALU.mult)
        vector.tensor_scalar(out=S["vsc"][:, 3:4], in0=S["iotp"][:], scalar1=129.0,
                             scalar2=32.0, op0=ALU.add, op1=ALU.mult)
        vector.drain()
        vector.reciprocal(out=S["vsc"][:, 0:1], in_=S["vsc"][:, 2:3])
        vector.reciprocal(out=S["vsc"][:, 1:2], in_=S["vsc"][:, 3:4])
        for blki in range(2):
            for h in range(H):
                vector.memset(S["vtm"][:, blki * 1040 + 65 * h + 64: blki * 1040 + 65 * h + 65], 1.0)
        vector.memset(S["awp"][:], 0.0)
        vector.memset(S["cb"][:, 0:1], EPS_LN)
        vector.memset(S["cb"][:, 1:2], 1.0)
        vector.memset(S["cb"][:, 2:3], 128.5)
        vector.memset(S["cb"][:, 3:4], EPS_ATT * EPS_LN)
        vector.fence_inc(dv.h, 1)
    EV(_v0); dv.v += 1
    t_consts = dv.v

    t_rd = {}
    for cc in range(KC):
        half = cc % 2
        wch = wa if half == 0 else wb
        def _wl(sync, cc=cc, half=half, wch=wch):
            if cc >= 2:
                sync.wait_ge(dv.h, t_rd[cc - 2])
            sync.dma_start(out=S["xsb"][:, half * D: (half + 1) * D],
                           in_=w_d["wfn"][cc * 128: (cc + 1) * 128, :]).then_inc(wch.h, 16)
        ES(_wl); wch.v += 16
        t_wl = wch.v

        def _wr(vector, cc=cc, half=half, t_wl=t_wl, wch=wch):
            vector.wait_ge(wch.h, t_wl)
            vector.tensor_reduce(out=S["wsv"][:, cc: cc + 1],
                                 in_=S["xsb"][:, half * D: (half + 1) * D],
                                 axis=AX.X, op=ALU.max, apply_absolute_value=True)
            vector.fence_inc(dv.h, 1)
        EV(_wr); dv.v += 1
        t_rd[cc] = dv.v

    def _wq1(vector):
        vector.drain()
        vector.tensor_scalar(out=S["wsv"][:], in0=S["wsv"][:], scalar1=EPS_FFN,
                             scalar2=1.0 / 127.0, op0=ALU.max, op1=ALU.mult)
        vector.drain()
        vector.reciprocal(out=S["vsc"][:, 16:24], in_=S["wsv"][:])
        vector.fence_inc(dv.h, 1)
    EV(_wq1); dv.v += 1
    t_wq1 = dv.v

    def _wq2(sync, _wbc=bc.v + 32):
        sync.wait_ge(dv.h, t_wq1)
        sync.dma_start(out=scr_ws[1:2, :].rearrange("o (c p) -> p (o c)", p=128),
                       in_=S["wsv"][:], allow_slow_non_contiguous=True).then_inc(bc.h, 16)
        sync.dma_start(out=scr_ws[0:1, :].rearrange("o (c p) -> p (o c)", p=128),
                       in_=S["vsc"][:, 16:24], allow_slow_non_contiguous=True).then_inc(bc.h, 16)
        sync.wait_ge(bc.h, _wbc)
        sync.dma_start(out=S["wsb"][:], in_=scr_ws[1:2, :].broadcast_to([128, D])).then_inc(bc.h, 16)
        sync.dma_start(out=S["rwsb"][:], in_=scr_ws[0:1, :].broadcast_to([128, D])).then_inc(bc.h, 16)
    ES(_wq2); bc.v += 64
    t_wq2 = bc.v

    t_wq3c = {}
    for cc in range(KC):
        half = cc % 2
        wch = wa if half == 0 else wb
        def _wl2(sync, cc=cc, half=half, wch=wch):
            if cc >= 2:
                sync.wait_ge(dv.h, t_wq3c[cc - 2])
            sync.dma_start(out=S["xsb"][:, half * D: (half + 1) * D],
                           in_=w_d["wft"][cc * 128: (cc + 1) * 128, :]).then_inc(wch.h, 16)
        ES(_wl2); wch.v += 16
        t_wl2 = wch.v

        def _wq3(vector, cc=cc, half=half, t_wl2=t_wl2, t_wq2=t_wq2, wch=wch):
            vector.wait_ge(wch.h, t_wl2)
            vector.wait_ge(bc.h, t_wq2)
            src = S["xsb"][:, half * D: (half + 1) * D]
            y, r, d_ = S["ystrip"][:, 0:D], S["sd"][:, 0:D], S["d0"][:]
            vector.tensor_tensor(out=y, in0=src, in1=S["rwsb"][:], op=ALU.mult)
            vector.drain()
            vector.tensor_scalar(out=r, in0=y, scalar1=128.5, scalar2=None, op0=ALU.add)
            vector.drain()
            for k in range(7, -1, -1):
                sv = float(1 << k)
                vector.tensor_scalar(out=d_, in0=r, scalar1=sv, scalar2=-sv,
                                     op0=ALU.is_ge, op1=ALU.mult)
                vector.drain()
                vector.tensor_tensor(out=r, in0=r, in1=d_, op=ALU.add)
                vector.drain()
            vector.scalar_tensor_tensor(out=S["wf"][:, cc * D: (cc + 1) * D],
                                        in0=y, scalar=0.5, in1=r, op0=ALU.add, op1=ALU.subtract)
            vector.fence_inc(dv.h, 1)
        EV(_wq3); dv.v += 1
        t_wq3c[cc] = dv.v
    tk0_wq3 = dv.v

    tk = {"xsb_free_ac": 0, "xsb_free_dv": 0, "xn_t_free_pe": 0, "xn_t_free_dv": 0,
          "xrl_free": 0, "epf_free_ac": 0, "strip_free": 0,
          "dup": {}, "cp": {}}

    def st(tb, j):
        return S["stats"][:, tb * 8 + j: tb * 8 + j + 1]

    units = [(b, si) for b in range(BC) for si in range(NSB)][:n_units]
    for b, si in units:
        if True:
            c = si
            tok0 = si * SEG
            pe_prev = pe.v

            def _ph0(sync, b=b, tok0=tok0, w1=tk["xsb_free_ac"],
                     w2=max(tk["xsb_free_dv"], tk0_wq3 if (b == 0 and si == 0) else 0)):
                if w1:
                    sync.wait_ge(ac.h, w1)
                if w2:
                    sync.wait_ge(dv.h, w2)
                sync.dma_start(
                    out=S["xsb"][:].rearrange("p (tb d) -> p tb d", d=D),
                    in_=x_d[b, tok0: tok0 + SEG, :].rearrange("(tb p) d -> p tb d", p=128),
                ).then_inc(xl.h, 16)
            ES(_ph0); xl.v += 16
            t_x = xl.v

            def _p1v(vector, t_x=t_x):
                W(vector, (xl, t_x), (dv, t_consts), (bc, t_wq2))
                for tb in range(2):
                    vector.tensor_reduce(out=S["vsc"][:, 4 + tb: 5 + tb],
                                         in_=S["xsb"][:, tb * D: (tb + 1) * D], axis=AX.X, op=ALU.add)
                vector.drain()
                for tb in range(2):
                    vector.tensor_scalar(out=st(tb, 0), in0=S["vsc"][:, 4 + tb: 5 + tb],
                                         scalar1=-1.0 / D, scalar2=None, op0=ALU.mult)
                vector.fence_inc(dv.h, 1)
            EV(_p1v); dv.v += 1
            t_mean = dv.v

            def _p1a(scalar, t_x=t_x, t_mean=t_mean, wdv=tk["strip_free"]):
                W(scalar, (xl, t_x), (dv, t_mean))
                if wdv:
                    scalar.wait_ge(dv.h, wdv)
                for tb in range(2):
                    scalar.activation(out=S["ep_f" if tb == 0 else "ep_g"][:],
                                      in_=S["xsb"][:, tb * D: (tb + 1) * D],
                                      func=AF.Square, accum_out=S["vsc"][:, 6 + tb: 7 + tb])
                scalar.fence_inc(ac.h, 1)
            EA(_p1a); ac.v += 1
            t_sq = ac.v

            def _p1v2(vector, t_sq=t_sq):
                W(vector, (ac, t_sq))
                for tb in range(2):
                    vector.tensor_scalar(out=S["vsc"][:, 8 + tb: 9 + tb], in0=st(tb, 0),
                                         scalar1=st(tb, 0), scalar2=None, op0=ALU.mult)
                    vector.tensor_scalar(out=S["vsc"][:, 10 + tb: 11 + tb],
                                         in0=S["vsc"][:, 6 + tb: 7 + tb],
                                         scalar1=1.0 / D, scalar2=None, op0=ALU.mult)
                vector.drain()
                for tb in range(2):
                    vector.tensor_tensor(out=S["vsc"][:, 12 + tb: 13 + tb],
                                         in0=S["vsc"][:, 10 + tb: 11 + tb],
                                         in1=S["vsc"][:, 8 + tb: 9 + tb], op=ALU.subtract)
                vector.fence_inc(dv.h, 1)
            EV(_p1v2); dv.v += 1
            t_var = dv.v

            def _p1a2(scalar, t_var=t_var):
                W(scalar, (dv, t_var))
                for tb in range(2):
                    scalar.activation(out=S["vsc"][:, 14 + tb: 15 + tb],
                                      in_=S["vsc"][:, 12 + tb: 13 + tb], func=AF.Ln, bias=S["cb"][:, 0:1])
                    scalar.activation(out=S["vsc"][:, 18 + tb: 19 + tb],
                                      in_=S["vsc"][:, 12 + tb: 13 + tb], func=AF.Ln,
                                      scale=1.0 + EPS_ATT, bias=S["cb"][:, 3:4])
                scalar.drain()
                for tb in range(2):
                    scalar.activation(out=st(tb, 1), in_=S["vsc"][:, 14 + tb: 15 + tb],
                                      func=AF.Exp, scale=-0.5)
                    scalar.activation(out=st(tb, 2), in_=S["vsc"][:, 18 + tb: 19 + tb],
                                      func=AF.Exp, scale=-0.5)
                scalar.fence_inc(ac.h, 1)
            EA(_p1a2); ac.v += 1
            t_sd_ = ac.v

            def _p1v3(vector, t_sd_=t_sd_):
                W(vector, (ac, t_sd_))
                for tb in range(2):
                    vector.tensor_scalar(out=st(tb, 5), in0=st(tb, 0), scalar1=st(tb, 1),
                                         scalar2=None, op0=ALU.mult)
                vector.fence_inc(dv.h, 1)
            EV(_p1v3); dv.v += 1
            t_istd = dv.v

            t_xqf_prev = tk["strip_free"]
            t_xn = {}
            for tb in range(2):
                def _xn(scalar, tb=tb, t_istd=t_istd,
                        wpe=tk["xn_t_free_pe"], wdv=tk["xn_t_free_dv"]):
                    W(scalar, (dv, t_istd))
                    if wpe:
                        scalar.wait_ge(pe.h, wpe)
                    if wdv:
                        scalar.wait_ge(dv.h, wdv)
                    scalar.activation(out=S["xn_t"][:], in_=S["xsb"][:, tb * D: (tb + 1) * D],
                                      func=AF.Identity, bias=st(tb, 5), scale=st(tb, 1))
                    scalar.fence_inc(ac.h, 1)
                EA(_xn); ac.v += 1
                t_xn[tb] = ac.v

                def _am(vector, tb=tb, t_x2=t_xn[tb]):
                    W(vector, (ac, t_x2))
                    vector.tensor_reduce(out=S["vsc"][:, 16 + tb: 17 + tb], in_=S["xn_t"][:],
                                         axis=AX.X, op=ALU.max, apply_absolute_value=True)
                    vector.drain()
                    vector.tensor_scalar(out=st(tb, 4), in0=S["vsc"][:, 16 + tb: 17 + tb],
                                         scalar1=EPS_FFN, scalar2=1.0 / 127.0,
                                         op0=ALU.max, op1=ALU.mult)
                    vector.drain()
                    vector.reciprocal(out=st(tb, 3), in_=st(tb, 4))
                    vector.drain()
                    vector.tensor_scalar(out=S["vsc"][:, 8:9], in0=st(tb, 2),
                                         scalar1=1.0, scalar2=None, op0=ALU.mult)
                    vector.tensor_scalar(out=S["vsc"][:, 9:10], in0=st(tb, 0),
                                         scalar1=st(tb, 2), scalar2=None, op0=ALU.mult)
                    vector.tensor_scalar(out=S["vsc"][:, 10:11], in0=st(tb, 1),
                                         scalar1=st(tb, 3), scalar2=None, op0=ALU.mult)
                    vector.tensor_scalar(out=S["vsc"][:, 11:12], in0=st(tb, 5),
                                         scalar1=st(tb, 3), scalar2=128.5,
                                         op0=ALU.mult, op1=ALU.add)
                    vector.fence_inc(dv.h, 1)
                EV(_am); dv.v += 1
                t_am = dv.v

                sbuf_sc = S["sd"][:, 0:1024]
                for var in range(3):
                    if var > 0:
                        pe_now = pe.v
                        def _sc(scalar, tb=tb, var=var, t_am=t_am, pe_now=pe_now):
                            W(scalar, (dv, t_am), (pe, pe_now))
                            if var == 1:
                                scalar.activation(out=S["sd"][:, 0:1024],
                                                  in_=S["xsb"][:, tb * D: (tb + 1) * D],
                                                  func=AF.Identity, scale=S["vsc"][:, 8:9],
                                                  bias=S["vsc"][:, 9:10])
                            else:
                                scalar.activation(out=S["sd"][:, 0:1024],
                                                  in_=S["xsb"][:, tb * D: (tb + 1) * D],
                                                  func=AF.Identity, scale=S["vsc"][:, 10:11],
                                                  bias=S["vsc"][:, 11:12])
                            scalar.fence_inc(ac.h, 1)
                        EA(_sc); ac.v += 1
                        t_src = ac.v
                        src = sbuf_sc
                    else:
                        t_src = t_xn[tb]
                        src = S["xn_t"][:]
                    for grp in range(2):
                        bank = "a" if ((var * 2 + grp) % 2 == 0) else "b"
                        psu = PS[bank]
                        def _tp(tensor, var=var, tb=tb, grp=grp, psu=psu, src=src,
                                t_src=t_src, bw=psfree[bank]):
                            W(tensor, (ac, t_src))
                            if bw is not None:
                                W(tensor, bw)
                            for j in range(4):
                                kc = grp * 4 + j
                                tensor.transpose(out=psu[:, j * 128: (j + 1) * 128],
                                                 in_=src[:, kc * 128: (kc + 1) * 128],
                                                 identity=S["ident"][:])
                            tensor.fence_inc(pe.h, 1)
                        EP(_tp); pe.v += 1
                        t_tp = pe.v

                        if var == 0:
                            def _ev(scalar, tb=tb, grp=grp, psu=psu, t_tp=t_tp):
                                W(scalar, (pe, t_tp))
                                for j in range(4):
                                    kc = grp * 4 + j
                                    scalar.copy(
                                        out=S["xnf"][:, kc * SEG + tb * 128: kc * SEG + tb * 128 + 128],
                                        in_=psu[:, j * 128: (j + 1) * 128])
                                scalar.fence_inc(ac.h, 1)
                            EA(_ev); ac.v += 1
                            psfree[bank] = (ac, ac.v)
                        elif var == 1:
                            def _ev(scalar, tb=tb, grp=grp, psu=psu, t_tp=t_tp):
                                W(scalar, (pe, t_tp))
                                for j in range(4):
                                    kc = grp * 4 + j
                                    scalar.activation(
                                        out=S["xn2f"][:, kc * SEG + tb * 128: kc * SEG + tb * 128 + 128],
                                        in_=psu[:, j * 128: (j + 1) * 128],
                                        func=AF.Identity, scale=S["nwv"][:, kc: kc + 1])
                                scalar.fence_inc(ac.h, 1)
                            EA(_ev); ac.v += 1
                            psfree[bank] = (ac, ac.v)
                        else:
                            def _ev(scalar, tb=tb, grp=grp, psu=psu, t_tp=t_tp, ws=t_xqf_prev,
                                    wso=tk.get("ep_a_free", 0)):
                                W(scalar, (pe, t_tp))
                                if ws:
                                    scalar.wait_ge(dv.h, ws)
                                if wso:
                                    scalar.wait_ge(so.h, wso)
                                scalar.copy(
                                    out=S["ystrip"][:, tb * 1024 + grp * 512: tb * 1024 + grp * 512 + 512],
                                    in_=psu[:])
                                scalar.fence_inc(ac.h, 1)
                            EA(_ev); ac.v += 1
                            psfree[bank] = (ac, ac.v)
                tk["xn_t_free_pe"] = pe.v
                tk["xn_t_free_dv"] = t_am
            tk["xsb_free_ac"] = 0
            t_y = ac.v
            def _strip(vector, t_y=t_y, we=tk["epf_free_ac"], wg=tk.get("gwr", 0)):
                W(vector, (ac, t_y))
                if we:
                    vector.wait_ge(ac.h, we)
                if wg:
                    vector.wait_ge(gw.h, wg)
                tt = S["ystrip"][:, 0:2048]
                rr = S["sd"][:, 0:2048]
                dd = S["zone"][:, 4096:6144]
                vector.tensor_copy(out=rr, in_=tt)
                for k in range(7, -1, -1):
                    sv = float(1 << k)
                    vector.drain()
                    vector.tensor_scalar(out=dd, in0=rr, scalar1=sv, scalar2=-sv,
                                         op0=ALU.is_ge, op1=ALU.mult)
                    vector.drain()
                    vector.tensor_tensor(out=rr, in0=rr, in1=dd, op=ALU.add)
                vector.drain()
                vector.tensor_scalar(out=tt, in0=tt, scalar1=-128.0, scalar2=None, op0=ALU.add)
                vector.drain()
                for tb in range(2):
                    tt_src = S["ystrip"][:, tb * 1024: (tb + 1) * 1024].rearrange("p (kc t) -> p kc t", kc=8)
                    rr_src = S["sd"][:, tb * 1024: (tb + 1) * 1024].rearrange("p (kc t) -> p kc t", kc=8)
                    o_ap = S["xqf"][:].rearrange("p (kc t) -> p kc t", kc=8)[:, :, tb * 128: tb * 128 + 128]
                    vector.tensor_tensor(out=o_ap, in0=tt_src, in1=rr_src, op=ALU.subtract)
                vector.fence_inc(dv.h, 1)
            EV(_strip); dv.v += 1
            tk["strip_free"] = dv.v
            t_xqf = dv.v

            if ph < 2:
                continue
            t_wq_ready = t_wload
            elu_fr = {"1": 0, "2": 0}
            t_xn2 = ac.v
            t_xnf = ac.v

            for proj, wname, dst in (("q", "wq", "pqf"), ("k", "wk", "pkf")):
                for m in range(KC):
                    bank = "a" if (m % 2 == 0) else "b"
                    psu = PS[bank]
                    def _mm(tensor, wname=wname, m=m, psu=psu, bw=psfree[bank],
                            t_xn2=t_xn2, t_wq_ready=t_wq_ready):
                        W(tensor, (ac, t_xn2), (wt, t_wq_ready))
                        if bw is not None:
                            W(tensor, bw)
                        for k in range(KC):
                            tensor.matmul(out=psu[:, 0:SEG],
                                          lhsT=S[wname][:, k * D + m * 128: k * D + (m + 1) * 128],
                                          rhs=S["xn2f"][:, k * SEG: (k + 1) * SEG],
                                          start=(k == 0), stop=(k == KC - 1))
                        tensor.fence_inc(pe.h, 1)
                    EP(_mm); pe.v += 1
                    t_mm = pe.v

                    def _e1(vector, psu=psu, t_mm=t_mm, w1=elu_fr["1"]):
                        W(vector, (pe, t_mm))
                        if w1:
                            vector.wait_ge(ac.h, w1)
                        vector.tensor_scalar(out=S["elu1"][:, 0:SEG], in0=psu[:, 0:SEG],
                                             scalar1=0.0, scalar2=None, op0=ALU.min)
                        vector.fence_inc(dv.h, 1)
                    EV(_e1); dv.v += 1
                    t_e1 = dv.v

                    def _e2(scalar, t_e1=t_e1, w2=elu_fr["2"]):
                        W(scalar, (dv, t_e1))
                        if w2:
                            scalar.wait_ge(dv.h, w2)
                        scalar.activation(out=S["elu2"][:, 0:SEG], in_=S["elu1"][:, 0:SEG], func=AF.Exp)
                        scalar.fence_inc(ac.h, 1)
                    EA(_e2); ac.v += 1
                    t_e2 = ac.v
                    elu_fr["1"] = t_e2

                    def _e3(vector, psu=psu, m=m, dst=dst, t_e2=t_e2):
                        W(vector, (ac, t_e2))
                        vector.scalar_tensor_tensor(out=S[dst][:, m * SEG: (m + 1) * SEG],
                                                    in0=psu[:, 0:SEG], scalar=0.0,
                                                    in1=S["elu2"][:, 0:SEG],
                                                    op0=ALU.max, op1=ALU.add)
                        vector.fence_inc(dv.h, 1)
                    EV(_e3); dv.v += 1
                    elu_fr["2"] = dv.v
                    psfree[bank] = (dv, dv.v)
            t_pq = dv.v
            t_pk = dv.v

            for m in range(KC):
                bank = "a" if (m % 2 == 0) else "b"
                psu = PS[bank]
                def _mmu(tensor, m=m, psu=psu, bw=psfree[bank], t_xnf=t_xnf, t_wq_ready=t_wq_ready):
                    W(tensor, (ac, t_xnf), (wt, t_wq_ready))
                    if bw is not None:
                        W(tensor, bw)
                    for k in range(KC):
                        tensor.matmul(out=psu[:, 0:SEG],
                                      lhsT=S["wu"][:, k * D + m * 128: k * D + (m + 1) * 128],
                                      rhs=S["xnf"][:, k * SEG: (k + 1) * SEG],
                                      start=(k == 0), stop=(k == KC - 1))
                    tensor.fence_inc(pe.h, 1)
                EP(_mmu); pe.v += 1
                t_mm = pe.v

                def _evu(scalar, m=m, psu=psu, t_mm=t_mm):
                    W(scalar, (pe, t_mm))
                    scalar.copy(out=S["uf"][:, m * SEG: (m + 1) * SEG], in_=psu[:, 0:SEG])
                    scalar.fence_inc(ac.h, 1)
                EA(_evu); ac.v += 1
                psfree[bank] = (ac, ac.v)
            t_uf = ac.v

            for tb in range(2):
                for half in range(2):
                    bank = "a" if ((tb * 2 + half) % 2 == 0) else "b"
                    psu = PS[bank]
                    def _mmv(tensor, tb=tb, half=half, psu=psu, bw=psfree[bank],
                             t_xn2=t_xn2, t_wq_ready=t_wq_ready):
                        W(tensor, (ac, t_xn2), (wt, t_wq_ready))
                        if bw is not None:
                            W(tensor, bw)
                        for k in range(KC):
                            tensor.matmul(out=psu[:],
                                          lhsT=S["xn2f"][:, k * SEG + tb * 128: k * SEG + (tb + 1) * 128],
                                          rhs=S["wv"][:, k * D + half * 512: k * D + half * 512 + 512],
                                          start=(k == 0), stop=(k == KC - 1))
                        tensor.fence_inc(pe.h, 1)
                    EP(_mmv); pe.v += 1
                    t_mm = pe.v

                    def _evv(vector, tb=tb, half=half, psu=psu, t_mm=t_mm):
                        W(vector, (pe, t_mm))
                        base = tb * 1040 + half * 520
                        vector.tensor_copy(
                            out=S["vtm"][:, base: base + 520].rearrange(
                                "p (h f) -> p h f", f=65)[:, :, 0:64],
                            in_=psu[:].rearrange("p (h f) -> p h f", f=64))
                        vector.fence_inc(dv.h, 1)
                    EV(_evv); dv.v += 1
                    psfree[bank] = (dv, dv.v)
            t_vtm = dv.v

            for step in range(4):
                tb, grp = divmod(step, 2)
                bank = "a" if (step % 2 == 0) else "b"
                psu = PS[bank]
                def _tpk(tensor, tb=tb, grp=grp, psu=psu, bw=psfree[bank], t_pk=t_pk):
                    W(tensor, (dv, t_pk))
                    if bw is not None:
                        W(tensor, bw)
                    pb = psu[:].bitcast(BF16)
                    for j in range(4):
                        rho = grp * 4 + j
                        tensor.transpose(out=pb[:, j * 128: (j + 1) * 128],
                                         in_=S["pkf"][:, rho * SEG + tb * 128: rho * SEG + (tb + 1) * 128],
                                         identity=S["identb"][:])
                    tensor.fence_inc(pe.h, 1)
                EP(_tpk); pe.v += 1
                t_mm = pe.v

                def _etk(scalar, tb=tb, grp=grp, psu=psu, t_mm=t_mm):
                    W(scalar, (pe, t_mm))
                    pb = psu[:].bitcast(BF16)
                    for j in range(4):
                        rho = grp * 4 + j
                        scalar.copy(out=S["pktm"][:, tb * 1024 + rho * 128: tb * 1024 + (rho + 1) * 128],
                                    in_=pb[:, j * 128: (j + 1) * 128])
                    scalar.fence_inc(ac.h, 1)
                EA(_etk); ac.v += 1
                psfree[bank] = (ac, ac.v)
            t_pktm = ac.v

            def _mst(vector, c=c, t_xnf=t_xnf, wpe=pe_prev):
                W(vector, (ac, t_xnf))
                if wpe:
                    vector.wait_ge(pe.h, wpe)
                with nc.allow_low_precision(reason="m_stack sums stored bf16 as matmul operand"):
                    for kc in range(KC):
                        vector.tensor_reduce(out=S["mst"][:, kc * 8 + c: kc * 8 + c + 1],
                                             in_=S["xnf"][:, kc * SEG: (kc + 1) * SEG],
                                             axis=AX.X, op=ALU.add)
                vector.fence_inc(dv.h, 1)
            EV(_mst); dv.v += 1
            t_mst = dv.v

            if ph < 3:
                continue
            def _s2t(tensor, t_xnf=t_xnf, t_uf=t_uf, bw=psfree["a"]):
                W(tensor, (ac, max(t_xnf, t_uf)))
                if bw is not None:
                    W(tensor, bw)
                for k in range(KC):
                    tensor.matmul(out=ps_a[:, 0:256],
                                  lhsT=S["xnf"][:, k * SEG: k * SEG + 128],
                                  rhs=S["uf"][:, k * SEG: (k + 1) * SEG],
                                  start=(k == 0), stop=(k == KC - 1))
                for k in range(KC):
                    tensor.matmul(out=ps_a[:, 256:384],
                                  lhsT=S["xnf"][:, k * SEG + 128: (k + 1) * SEG],
                                  rhs=S["uf"][:, k * SEG + 128: (k + 1) * SEG],
                                  start=(k == 0), stop=(k == KC - 1))
                tensor.fence_inc(pe.h, 1)
            EP(_s2t); pe.v += 1
            t_s2t = pe.v

            def _s2e(scalar, t_s2t=t_s2t):
                W(scalar, (pe, t_s2t))
                scalar.copy(out=S["s2m"][:, 0:384], in_=ps_a[:, 0:384])
                scalar.fence_inc(ac.h, 1)
            EA(_s2e); ac.v += 1
            t_s2e = ac.v

            def _s2mk(vector, t_s2e=t_s2e):
                W(vector, (ac, t_s2e))
                vector.tensor_tensor(out=S["s2m"][:, 0:128], in0=S["s2m"][:, 0:128],
                                     in1=S["maskA"][:], op=ALU.mult)
                vector.tensor_tensor(out=S["s2m"][:, 256:384], in0=S["s2m"][:, 256:384],
                                     in1=S["maskA"][:], op=ALU.mult)
                vector.fence_inc(dv.h, 1)
            EV(_s2mk); dv.v += 1
            t_s2m = dv.v
            psfree["a"] = (ac, t_s2e)

            def _scmm(tensor, c=c, t_s2m=t_s2m, t_mst=t_mst, t_uf=t_uf, bw=psfree["b"]):
                W(tensor, (dv, max(t_s2m, t_mst)), (ac, t_uf))
                if bw is not None:
                    W(tensor, bw)
                ones = S["vtm"][:, 64:65]
                tensor.matmul(out=ps_b[:, 8:9], lhsT=S["s2m"][:, 0:128], rhs=ones,
                              start=True, stop=True)
                tensor.matmul(out=ps_b[:, 24:25], lhsT=S["s2m"][:, 128:256], rhs=ones,
                              start=True, stop=False, skip_group_check=True)
                tensor.matmul(out=ps_b[:, 24:25], lhsT=S["s2m"][:, 256:384], rhs=ones,
                              start=False, stop=True, skip_group_check=True)
                if c > 0:
                    for tb in range(2):
                        for k in range(KC):
                            tensor.matmul(out=ps_b[:, tb * 16: tb * 16 + c],
                                          lhsT=S["uf"][:, k * SEG + tb * 128: k * SEG + (tb + 1) * 128],
                                          rhs=S["mst"][:, k * 8: k * 8 + c],
                                          start=(k == 0), stop=(k == KC - 1),
                                          skip_group_check=True)
                tensor.fence_inc(pe.h, 1)
            EP(_scmm); pe.v += 1
            t_scmm = pe.v

            def _scev(vector, c=c, t_scmm=t_scmm):
                W(vector, (pe, t_scmm))
                for tb in range(2):
                    if c > 0:
                        vector.tensor_scalar(out=S["sc"][:, tb * 16: tb * 16 + c],
                                             in0=ps_b[:, tb * 16: tb * 16 + c],
                                             scalar1=1.0 / 8192.0, scalar2=None, op0=ALU.mult)
                    if c < 8:
                        vector.memset(S["sc"][:, tb * 16 + c: tb * 16 + 8], NEG)
                    vector.tensor_scalar(out=S["sc"][:, tb * 16 + 8: tb * 16 + 9],
                                         in0=ps_b[:, tb * 16 + 8: tb * 16 + 9],
                                         scalar1=S["vsc"][:, tb: tb + 1], scalar2=None, op0=ALU.mult)
                vector.drain()
                for tb in range(2):
                    vector.tensor_reduce(out=S["vsc"][:, 26 + tb: 27 + tb],
                                         in_=S["sc"][:, tb * 16: tb * 16 + 9],
                                         axis=AX.X, op=ALU.max, negate=True)
                vector.fence_inc(dv.h, 1)
            EV(_scev); dv.v += 1
            t_scev = dv.v
            psfree["b"] = (dv, t_scev)

            def _scexp(scalar, t_scev=t_scev):
                W(scalar, (dv, t_scev))
                for tb in range(2):
                    scalar.activation(out=S["aw"][:, tb * 16: tb * 16 + 9],
                                      in_=S["sc"][:, tb * 16: tb * 16 + 9], func=AF.Exp,
                                      bias=S["vsc"][:, 26 + tb: 27 + tb],
                                      accum_out=S["vsc"][:, 28 + tb: 29 + tb])
                scalar.fence_inc(ac.h, 1)
            EA(_scexp); ac.v += 1
            t_sce = ac.v

            def _scn(vector, t_sce=t_sce):
                W(vector, (ac, t_sce))
                for tb in range(2):
                    vector.reciprocal(out=S["vsc"][:, 30 + tb: 31 + tb],
                                      in_=S["vsc"][:, 28 + tb: 29 + tb])
                vector.drain()
                for tb in range(2):
                    vector.tensor_scalar(out=S["aw"][:, tb * 16: tb * 16 + 9],
                                         in0=S["aw"][:, tb * 16: tb * 16 + 9],
                                         scalar1=S["vsc"][:, 30 + tb: 31 + tb],
                                         scalar2=None, op0=ALU.mult)
                vector.drain()
                for tb in range(2):
                    vector.tensor_copy(out=st(tb, 6), in_=S["aw"][:, tb * 16 + 8: tb * 16 + 9])
                vector.fence_inc(dv.h, 1)
            EV(_scn); dv.v += 1
            t_aw = dv.v

            t_gt_prev = tk.get("awp_free", 0)
            for tb in range(2):
                def _gtp0(vector, tb=tb, t_aw=t_aw, wpe=t_gt_prev):
                    W(vector, (dv, t_aw))
                    if wpe:
                        vector.wait_ge(pe.h, wpe)
                    vector.tensor_copy(out=S["awp"][:, 0:9], in_=S["aw"][:, tb * 16: tb * 16 + 9])
                    vector.fence_inc(dv.h, 1)
                EV(_gtp0); dv.v += 1
                t_awp = dv.v

                def _gtp(tensor, t_awp=t_awp, bw=psfree["a"]):
                    W(tensor, (dv, t_awp))
                    if bw is not None:
                        W(tensor, bw)
                    tensor.transpose(out=ps_a[0:16, 0:128], in_=S["awp"][:, 0:16],
                                     identity=S["ident"][:])
                    tensor.fence_inc(pe.h, 1)
                EP(_gtp); pe.v += 1
                t_gt = pe.v
                t_gt_prev = t_gt
                tk["awp_free"] = t_gt

                def _gte(vector, tb=tb, t_gt=t_gt, wg=tk.get("gwr", 0)):
                    W(vector, (pe, t_gt))
                    if wg:
                        vector.wait_ge(gw.h, wg)
                    vector.tensor_copy(out=S["gcm"][0:16, tb * 128: (tb + 1) * 128],
                                       in_=ps_a[0:16, 0:128])
                    vector.fence_inc(dv.h, 1)
                EV(_gte); dv.v += 1
                psfree["a"] = (dv, dv.v)
            t_gcm = dv.v

            def _gwr(gpsimd, tok0=tok0, t_gcm=t_gcm):
                W(gpsimd, (dv, t_gcm))
                gpsimd.dma_start(out=scr_g[:, tok0: tok0 + SEG],
                                 in_=S["gcm"][0:8, 0:SEG]).then_inc(gw.h, 16)
            EG(_gwr); gw.v += 16
            t_scrg = gw.v
            tk["gwr"] = t_scrg

            if ph < 4:
                continue
            for bl in range(2):
                slot = si * 2 + bl

                def _smm(tensor, bl=bl, slot=slot, t_pktm=t_pktm, t_vtm=t_vtm,
                         sw=psfree["scum"]):
                    W(tensor, (ac, t_pktm), (dv, t_vtm))
                    if sw is not None:
                        W(tensor, sw)
                    for h in range(H):
                        tensor.matmul(
                            out=ps_scum[0:64, 128 * h: 128 * h + 65],
                            lhsT=S["pktm"][:, bl * 1024 + (h // 2) * 128 + (h % 2) * 64:
                                           bl * 1024 + (h // 2) * 128 + (h % 2) * 64 + 64],
                            rhs=S["vtm"][:, bl * 1040 + 65 * h: bl * 1040 + 65 * h + 65],
                            start=(slot == 0 and h % 4 == 0), stop=(slot == 15),
                            skip_group_check=True)
                    tensor.fence_inc(pe.h, 1)
                EP(_smm); pe.v += 1
                t_smm = pe.v

                def _scp(vector, slot=slot, t_smm=t_smm):
                    W(vector, (pe, t_smm))
                    vector.tensor_copy(
                        out=S["stack"][0:64, slot * 1040: (slot + 1) * 1040],
                        in_=ps_scum[0:64, :].rearrange("p (h f) -> p h f", h=16)[:, :, 0:65])
                    vector.fence_inc(dv.h, 1)
                EV(_scp); dv.v += 1
                tk["cp"][slot] = dv.v
                psfree["scum"] = (dv, dv.v)

                dch = da if slot % 2 == 0 else db
                def _sdp(gpsimd, slot=slot, w=tk["cp"][slot], dch=dch):
                    W(gpsimd, (dv, w))
                    gpsimd.dma_start(out=S["stack"][64:128, slot * 1040: (slot + 1) * 1040],
                                     in_=S["stack"][0:64, slot * 1040: (slot + 1) * 1040]
                                     ).then_inc(dch.h, 16)
                EG(_sdp); dch.v += 16
                tk["dup"][slot] = (dch, dch.v)

                t_atm_prev = 0
                for g in range(4 if ph >= 41 else 0):
                    if ph == 410 and g > 0:
                        break
                    def _atmm(tensor, bl=bl, g=g, t_pq=t_pq, t_pk=t_pk, bw=psfree["a"]):
                        W(tensor, (dv, max(t_pq, t_pk)))
                        if bw is not None:
                            W(tensor, bw)
                        for j in range(4):
                            h = 4 * g + j
                            rho, par = h // 2, (h % 2) * 64
                            tensor.wait_ge(gp.h, 1)
                            tensor.matmul(
                                out=ps_a[:, j * 128: (j + 1) * 128],
                                lhsT=S["pkf"][par: par + 64, rho * SEG + bl * 128: rho * SEG + bl * 128 + 128],
                                rhs=S["pqf"][par: par + 64, rho * SEG + bl * 128: rho * SEG + bl * 128 + 128],
                                start=True, stop=True)
                        tensor.fence_inc(pe.h, 1)
                    EP(_atmm); pe.v += 1
                    t_at = pe.v

                    if ph == 410:
                        continue
                    def _atc(scalar, t_at=t_at, wm=t_atm_prev, t_scrg=t_scrg):
                        W(scalar, (pe, t_at), (gw, t_scrg))
                        if wm:
                            scalar.wait_ge(dv.h, wm)
                        scalar.copy(out=S["atb"][:], in_=ps_a[:])
                        scalar.fence_inc(ac.h, 1)
                    EA(_atc); ac.v += 1
                    t_atc = ac.v
                    psfree["a"] = (ac, t_atc)

                    def _atk(vector, t_atc=t_atc):
                        W(vector, (ac, t_atc))
                        vector.tensor_tensor(
                            out=S["atm"][:],
                            in0=S["atb"][:],
                            in1=S["maskA"][:].unsqueeze(1).broadcast_to([128, 4, 128]),
                            op=ALU.mult)
                        vector.fence_inc(dv.h, 1)
                    EV(_atk); dv.v += 1
                    t_atm = dv.v
                    t_atm_prev = t_atm

                    if (ph < 42 or ph == 411) and ph != 420:
                        continue
                    def _ii(tensor, bl=bl, g=g, slot=slot, t_atm=t_atm, t_vtm=t_vtm,
                            wdup=(tk["dup"].get(slot - 1, None)),
                            bwo=psfree["o"], bwb=psfree["b"]):
                        W(tensor, (dv, t_atm))
                        if slot > 0 and wdup is not None:
                            W(tensor, wdup)
                        if g == 0:
                            if bwo is not None:
                                W(tensor, bwo)
                            if bwb is not None:
                                W(tensor, bwb)
                        for j in range(4):
                            h = 4 * g + j
                            rho, par = h // 2, (h % 2) * 64
                            bk, off = out_off(h)
                            pout = PS[bk]
                            first = True
                            if slot > 0:
                                tensor.wait_ge(gp.h, 1)
                                tensor.matmul(
                                    out=pout[:, off: off + 65],
                                    lhsT=S["pqf"][par: par + 64,
                                                  rho * SEG + bl * 128: rho * SEG + bl * 128 + 128],
                                    rhs=S["stack"][par: par + 64,
                                                   (slot - 1) * 1040 + 65 * h: (slot - 1) * 1040 + 65 * h + 65],
                                    start=True, stop=False, skip_group_check=True)
                                first = False
                            tensor.wait_ge(gp.h, 1)
                            tensor.matmul(
                                out=pout[:, off: off + 65],
                                lhsT=S["atm"][:, j * 128: (j + 1) * 128],
                                rhs=S["vtm"][:, bl * 1040 + 65 * h: bl * 1040 + 65 * h + 65],
                                start=first, stop=True, skip_group_check=True)
                        tensor.fence_inc(pe.h, 1)
                    EP(_ii); pe.v += 1
                t_ii = pe.v

                if ph < 42 or ph in (411, 420):
                    continue
                def _dn(vector, bl=bl, t_ii=t_ii):
                    W(vector, (pe, t_ii))
                    vector.tensor_copy(
                        out=S["dvec"][:, 0:7],
                        in_=ps_o[:, 0:455].rearrange("p (h f) -> p h f", f=65)[:, :, 64:65])
                    vector.tensor_copy(
                        out=S["dvec"][:, 7:14],
                        in_=ps_o[:, 512:967].rearrange("p (h f) -> p h f", f=65)[:, :, 64:65])
                    vector.tensor_copy(
                        out=S["dvec"][:, 14:16],
                        in_=ps_b[:, 0:130].rearrange("p (h f) -> p h f", f=65)[:, :, 64:65])
                    vector.drain()
                    vector.tensor_scalar(out=S["dvec"][:], in0=S["dvec"][:], scalar1=EPS_ATT,
                                         scalar2=None, op0=ALU.add)
                    vector.drain()
                    vector.reciprocal(out=S["rvec"][:], in_=S["dvec"][:])
                    vector.drain()
                    for psu, base, h0, cnt in ((ps_o, 0, 0, 7), (ps_o, 512, 7, 7), (ps_b, 0, 14, 2)):
                        vector.tensor_tensor(
                            out=S["ar"][:].rearrange("p (b h f) -> p b h f", b=2, h=16)[
                                :, bl, h0: h0 + cnt, :],
                            in0=psu[:, base: base + cnt * 65].rearrange(
                                "p (h f) -> p h f", f=65)[:, :, 0:64],
                            in1=S["rvec"][:, h0: h0 + cnt].unsqueeze(2).broadcast_to(
                                [128, cnt, 64]),
                            op=ALU.mult)
                    vector.fence_inc(dv.h, 1)
                EV(_dn); dv.v += 1
                psfree["o"] = (dv, dv.v)
                psfree["b"] = (dv, dv.v)
            t_ar = dv.v

            if ph < 5 or ph in (40, 41, 42, 410, 411, 420):
                continue
            for step in range(4):
                bl, grp = divmod(step, 2)
                bank = "a" if (step % 2 == 0) else "b"
                psu = PS[bank]
                def _tar(tensor, bl=bl, grp=grp, psu=psu, bw=psfree[bank], t_ar=t_ar):
                    W(tensor, (dv, t_ar))
                    if bw is not None:
                        W(tensor, bw)
                    pb = psu[:].bitcast(BF16)
                    for j in range(4):
                        kc = grp * 4 + j
                        tensor.transpose(out=pb[:, j * 128: (j + 1) * 128],
                                         in_=S["ar"][:, bl * 1024 + kc * 128: bl * 1024 + (kc + 1) * 128],
                                         identity=S["identb"][:])
                    tensor.fence_inc(pe.h, 1)
                EP(_tar); pe.v += 1
                t_mm = pe.v

                def _ear(scalar, bl=bl, grp=grp, psu=psu, t_mm=t_mm):
                    W(scalar, (pe, t_mm))
                    pb = psu[:].bitcast(BF16)
                    for j in range(4):
                        kc = grp * 4 + j
                        scalar.copy(out=S["arf"][:, kc * SEG + bl * 128: kc * SEG + (bl + 1) * 128],
                                    in_=pb[:, j * 128: (j + 1) * 128])
                    scalar.fence_inc(ac.h, 1)
                EA(_ear); ac.v += 1
                psfree[bank] = (ac, ac.v)
            t_arf = ac.v
            t_ph4 = pe.v

            if ph < 6:
                continue
            t_xrl = 0

            t_pgt = {}
            t_hmmn = {}
            if c > 0:
                for n in range(c):
                    nb = n % 2
                    gch = ga if nb == 0 else gb
                    def _gld(gpsimd, n=n, nb=nb, tok0=tok0, t_scrg=t_scrg, gch=gch,
                             wfree=max(t_pgt.get(n - 2, tk.get("gstr_free", 0)), t_vtm)):
                        W(gpsimd, (gw, t_scrg))
                        if wfree:
                            gpsimd.wait_ge(dv.h, wfree)
                        gpsimd.dma_start(out=S["gstr"][:, nb * SEG: (nb + 1) * SEG],
                                         in_=scr_g[n: n + 1, tok0: tok0 + SEG].broadcast_to(
                                             [128, SEG])).then_inc(gch.h, 16)
                    EG(_gld); gch.v += 16
                    t_gld = gch.v

                    def _pgt(vector, n=n, nb=nb, t_gld=t_gld, t_pq=t_pq, gch=gch,
                             wpe=(t_ph4 if n < 2 else t_hmmn[n - 2])):
                        W(vector, (gch, t_gld), (dv, t_pq))
                        vector.wait_ge(pe.h, wpe)
                        for rho in range(KC):
                            vector.tensor_tensor(
                                out=pgt_buf(nb)[:, rho * SEG: (rho + 1) * SEG],
                                in0=S["pqf"][:, rho * SEG: (rho + 1) * SEG],
                                in1=S["gstr"][:, nb * SEG: (nb + 1) * SEG], op=ALU.mult)
                        vector.fence_inc(dv.h, 1)
                    EV(_pgt); dv.v += 1
                    t_pgt[n] = dv.v

                    def _hmm(tensor, n=n, nb=nb, c=c, t_p=t_pgt[n],
                             wdup=tk["dup"].get(2 * n + 1, None),
                             bwo=psfree["o"] if n == 0 else None,
                             bwa=psfree["a"] if n == 0 else None,
                             bwb=psfree["b"] if n == 0 else None):
                        W(tensor, (dv, t_p))
                        if wdup is not None:
                            W(tensor, wdup)
                        for bw in (bwo, bwa, bwb):
                            if bw is not None:
                                W(tensor, bw)
                        for h in range(H):
                            rho, par = h // 2, (h % 2) * 64
                            tensor.wait_ge(gp.h, 1)
                            for th in range(2):
                                if th == 0:
                                    o_ap = ps_o[:, 64 * h: 64 * h + 64]
                                else:
                                    o_ap = (ps_a[:, 64 * h: 64 * h + 64] if h < 8
                                            else ps_b[:, 64 * (h - 8): 64 * (h - 8) + 64])
                                tensor.matmul(
                                    out=o_ap,
                                    lhsT=pgt_buf(nb)[par: par + 64,
                                                     rho * SEG + th * 128:
                                                     rho * SEG + th * 128 + 128],
                                    rhs=S["stack"][par: par + 64,
                                                   (2 * n + 1) * 1040 + 65 * h:
                                                   (2 * n + 1) * 1040 + 65 * h + 64],
                                    start=(n == 0 and h % 8 == 0), stop=(n == c - 1),
                                    skip_group_check=True)
                        tensor.fence_inc(pe.h, 1)
                    EP(_hmm); pe.v += 1
                    t_hmmn[n] = pe.v
                t_hmm = pe.v
                tk["pgt_free"] = t_hmm
                tk["gstr_free"] = max(t_pgt.values()) if t_pgt else 0

                def _hbe(vector, t_hmm=t_hmm, t_scrg=t_scrg):
                    W(vector, (pe, t_hmm), (gw, t_scrg))
                    vector.tensor_copy(out=S["hb"][:, 0:512], in_=ps_a[:])
                    vector.tensor_copy(out=S["hb"][:, 512:1024], in_=ps_b[:])
                    vector.fence_inc(dv.h, 1)
                EV(_hbe); dv.v += 1
                psfree["a"] = (dv, dv.v)
                psfree["b"] = (dv, dv.v)
                t_hb = dv.v
            else:
                t_hb = 0

            for tb in range(2):
                for half in range(2):
                    bank = "a" if (half == 0) else "b"
                    psu = PS[bank]
                    def _apm(tensor, tb=tb, half=half, psu=psu, bw=psfree[bank],
                             t_arf=t_arf, t_wq_ready=t_wq_ready):
                        W(tensor, (ac, t_arf), (wt, t_wq_ready))
                        if bw is not None:
                            W(tensor, bw)
                        for k in range(KC):
                            tensor.matmul(out=psu[:],
                                          lhsT=S["arf"][:, k * SEG + tb * 128: k * SEG + (tb + 1) * 128],
                                          rhs=S["wo"][:, k * D + half * 512: k * D + half * 512 + 512],
                                          start=(k == 0), stop=(k == KC - 1))
                        tensor.fence_inc(pe.h, 1)
                    EP(_apm); pe.v += 1
                    t_mm = pe.v

                    def _ape(scalar, tb=tb, half=half, psu=psu, t_mm=t_mm,
                             wso=tk.get("ep_a_free", 0)):
                        W(scalar, (pe, t_mm))
                        if wso:
                            scalar.wait_ge(so.h, wso)
                        scalar.activation(out=S["ep_a"][:, half * 512: half * 512 + 512],
                                          in_=psu[:], func=AF.Identity, scale=st(tb, 6))
                        scalar.fence_inc(ac.h, 1)
                    EA(_ape); ac.v += 1
                    psfree[bank] = (ac, ac.v)
                t_apa = ac.v

                for half in range(2):
                    bank = "a" if (half == 0) else "b"
                    psu = PS[bank]
                    def _ffm(tensor, tb=tb, half=half, psu=psu, bw=psfree[bank],
                             t_xqf=t_xqf):
                        W(tensor, (dv, t_xqf))
                        if bw is not None:
                            W(tensor, bw)
                        for k in range(KC):
                            tensor.matmul(out=psu[:],
                                          lhsT=S["xqf"][:, k * SEG + tb * 128: k * SEG + (tb + 1) * 128],
                                          rhs=S["wf"][:, k * D + half * 512: k * D + half * 512 + 512],
                                          start=(k == 0), stop=(k == KC - 1))
                        tensor.fence_inc(pe.h, 1)
                    EP(_ffm); pe.v += 1
                    t_mm = pe.v

                    def _ffe(vector, tb=tb, half=half, psu=psu, t_mm=t_mm,
                             wsl=tk.get("ep_f_free", 0)):
                        W(vector, (pe, t_mm), (bc, t_wq2))
                        if wsl:
                            vector.wait_ge(ac.h, wsl)
                        vector.tensor_tensor(out=S["ep_f"][:, half * 512: half * 512 + 512],
                                             in0=psu[:], in1=S["wsb"][:, half * 512: half * 512 + 512],
                                             op=ALU.mult)
                        vector.fence_inc(dv.h, 1)
                    EV(_ffe); dv.v += 1
                    psfree[bank] = (dv, dv.v)
                t_f1 = dv.v

                def _f2(vector, tb=tb, t_f1=t_f1):
                    vector.drain()
                    vector.scalar_tensor_tensor(out=S["ep_f"][:], in0=S["ep_f"][:],
                                                scalar=st(tb, 4), in1=S["bbc"][:],
                                                op0=ALU.mult, op1=ALU.add)
                    vector.fence_inc(dv.h, 1)
                EV(_f2); dv.v += 1
                t_f2 = dv.v

                def _sil(scalar, t_f2=t_f2):
                    W(scalar, (dv, t_f2))
                    scalar.activation(out=S["ep_g"][:], in_=S["ep_f"][:], func=AF.Exp,
                                      scale=-1.0)
                    scalar.drain()
                    scalar.activation(out=S["ep_g"][:], in_=S["ep_g"][:], func=AF.Ln,
                                      bias=S["cb"][:, 1:2])
                    scalar.drain()
                    scalar.activation(out=S["ep_g"][:], in_=S["ep_g"][:], func=AF.Exp,
                                      scale=-1.0)
                    scalar.fence_inc(ac.h, 1)
                EA(_sil); ac.v += 1
                t_sile = ac.v
                tk["ep_f_free"] = t_sile
                tk["epf_free_ac"] = t_sile

                def _sil2(vector, t_sile=t_sile):
                    W(vector, (ac, t_sile))
                    vector.tensor_tensor(out=S["ep_g"][:], in0=S["ep_g"][:],
                                         in1=S["ep_f"][:], op=ALU.mult)
                    vector.fence_inc(dv.h, 1)
                EV(_sil2); dv.v += 1
                t_sil = dv.v

                def _gg(vector, tb=tb, t_aw=t_aw, ws1=tk.get("s1_done", 0)):
                    if ws1:
                        vector.wait_ge(ac.h, ws1)
                    vector.tensor_scalar(out=S["vsc"][:, 24:25], in0=st(tb, 6),
                                         scalar1=st(tb, 1), scalar2=None, op0=ALU.mult)
                    vector.tensor_scalar(out=S["vsc"][:, 25:26], in0=st(tb, 6),
                                         scalar1=st(tb, 5), scalar2=None, op0=ALU.mult)
                    vector.fence_inc(dv.h, 1)
                EV(_gg); dv.v += 1
                t_gg = dv.v

                def _s1(scalar, tb=tb, t_gg=t_gg):
                    W(scalar, (dv, t_gg))
                    scalar.activation(out=S["ep_e"][:], in_=S["xsb"][:, tb * D: (tb + 1) * D],
                                      func=AF.Identity, scale=S["vsc"][:, 24:25],
                                      bias=S["vsc"][:, 25:26])
                    scalar.fence_inc(ac.h, 1)
                EA(_s1); ac.v += 1
                t_s1 = ac.v
                tk["s1_done"] = t_s1
                tk["xsb_free_ac"] = t_s1

                def _epi(vector, tb=tb, c=c, t_s1=t_s1, t_apa=t_apa, t_sil=t_sil, t_hb=t_hb):
                    W(vector, (ac, max(t_s1, t_apa)))
                    vector.tensor_tensor(out=S["ep_e"][:], in0=S["ep_e"][:], in1=S["ep_a"][:],
                                         op=ALU.add)
                    vector.drain()
                    if c > 0:
                        if tb == 0:
                            vector.tensor_tensor(out=S["ep_e"][:], in0=S["ep_e"][:],
                                                 in1=ps_o[:], op=ALU.add)
                        else:
                            vector.tensor_tensor(out=S["ep_e"][:], in0=S["ep_e"][:],
                                                 in1=S["hb"][:], op=ALU.add)
                        vector.drain()
                    vector.tensor_tensor(out=S["ep_f"][:], in0=S["ep_e"][:], in1=S["ep_g"][:],
                                         op=ALU.mult)
                    vector.drain()
                    vector.tensor_tensor(out=S["ep_a"][:], in0=S["xsb"][:, tb * D: (tb + 1) * D],
                                         in1=S["ep_f"][:], op=ALU.add)
                    vector.fence_inc(dv.h, 1)
                EV(_epi); dv.v += 1
                t_epi = dv.v
                tk["xsb_free_dv"] = t_epi
                if tb == 0 and c > 0:
                    psfree["o"] = (dv, t_epi)

                def _out(gpsimd, b=b, tok0=tok0, tb=tb, t_epi=t_epi):
                    W(gpsimd, (dv, t_epi))
                    gpsimd.dma_start(out=out_d[b, tok0 + tb * 128: tok0 + (tb + 1) * 128, :],
                                     in_=S["ep_a"][:]).then_inc(so.h, 16)
                EG(_out); so.v += 16
                tk["ep_a_free"] = so.v

    if dbg_d:
        t_all = (dv.v, ac.v, pe.v)
        def _dbg(gpsimd, t_all=t_all):
            gpsimd.wait_ge(dv.h, t_all[0])
            gpsimd.wait_ge(ac.h, t_all[1])
            gpsimd.wait_ge(pe.h, t_all[2])
            for nm, src in dbg_done:
                gpsimd.dma_start(out=dbg_d[nm], in_=src).then_inc(so.h, 16)
        EG(_dbg)
        so.v += 16 * len(dbg_done)

    if so.v == 0:
        def _dummy(gpsimd):
            gpsimd.wait_ge(bc.h, 32)
            gpsimd.dma_start(out=out_d[0, 0:128, :], in_=S["bbc"][:]).then_inc(so.h, 16)
        EG(_dummy); so.v += 16
    fin = so.v
    def _fin(gpsimd, fin=fin):
        gpsimd.wait_ge(so.h, fin)
    EG(_fin)

    class _EngP:
        def __init__(self, e, use_fence=True):
            self._e = e
            self._use_fence = use_fence
            self._last = None
        def __getattr__(self, name):
            attr = getattr(self._e, name)
            if callable(attr):
                def _call(*a, _attr=attr, **k):
                    r = _attr(*a, **k)
                    if hasattr(r, "then_inc"):
                        self._last = r
                    return r
                return _call
            return attr
        def fence_inc(self, sem, n):
            if self._use_fence:
                return self._last.then_inc(sem, n)
            return self._e.drain().then_inc(sem, n)

    for eng in ("sync", "gpsimd", "vector", "scalar", "tensor"):
        fns = [f for e, f in _PEND if e == eng]
        def _emit_all(engine, fns=fns, eng=eng):
            p = _EngP(engine, use_fence=(eng != "tensor"))
            for f in fns:
                f(p)
        getattr(blk, eng)(_emit_all)

    blk.__exit__(None, None, None)
    return nc


_NC_CACHE = {}


def kernel(L0_attn_x, L0_attn_norm_w, L0_attn_Wq, L0_attn_Wk, L0_attn_Wv,
           L0_attn_Wo, L0_attn_Wu, L0_ffn_weight, L0_ffn_bias):
    x = np.ascontiguousarray(np.asarray(L0_attn_x, dtype=np.float32))
    wq = np.ascontiguousarray(np.asarray(L0_attn_Wq, dtype=np.float32).T)
    wk = np.ascontiguousarray(np.asarray(L0_attn_Wk, dtype=np.float32).T)
    wv = np.ascontiguousarray(np.asarray(L0_attn_Wv, dtype=np.float32).T)
    wo = np.ascontiguousarray(np.asarray(L0_attn_Wo, dtype=np.float32).T)
    wu = np.ascontiguousarray(np.asarray(L0_attn_Wu, dtype=np.float32).T)
    wf = np.ascontiguousarray(np.asarray(L0_ffn_weight, dtype=np.float32).T)
    wfn = np.ascontiguousarray(np.asarray(L0_ffn_weight, dtype=np.float32))
    nw = np.ascontiguousarray(np.asarray(L0_attn_norm_w, dtype=np.float32))
    fb = np.ascontiguousarray(np.asarray(L0_ffn_bias, dtype=np.float32))

    if "nc" not in _NC_CACHE:
        _NC_CACHE["nc"] = build()
    nc = _NC_CACHE["nc"]

    in_maps = []
    for core in range(8):
        in_maps.append({
            "x": np.ascontiguousarray(x[2 * core: 2 * core + 2]),
            "wqt": wq, "wkt": wk, "wvt": wv, "wot": wo, "wut": wu,
            "wft": wf, "wfn": wfn, "nw": nw, "fb": fb,
        })
    res = run_bass_kernel_spmd(nc, in_maps, core_ids=list(range(8)))
    out = np.concatenate([np.asarray(r["out"]) for r in res.results], axis=0)
    return out.astype(np.float32)
